# version 02
# baseline (speedup 1.0000x reference)
import math
import numpy as np
from contextlib import ExitStack

import concourse.bass as bass
import concourse.mybir as mybir
from concourse.bass_utils import run_bass_kernel_spmd

F32 = mybir.dt.float32
BF16 = mybir.dt.bfloat16
AF = mybir.ActivationFunctionType
ALU = mybir.AluOpType
AX = mybir.AxisListType

D = 1024
T = 2048
NSEQ = 2
DC = D // 128
TB = 512
NTB = T // TB
LN_EPS = 1e-5


_CHAINED_OPS = frozenset({
    "memset", "memzero", "tensor_copy", "tensor_tensor", "tensor_add",
    "tensor_sub", "tensor_mul", "tensor_max", "tensor_scalar",
    "tensor_single_scalar", "tensor_scalar_add", "tensor_scalar_sub",
    "tensor_scalar_mul", "tensor_scalar_max", "tensor_scalar_min",
    "tensor_relu", "tensor_reduce", "reduce_max", "reduce_sum",
    "scalar_tensor_tensor", "tensor_tensor_scan", "select", "copy_predicated",
    "transpose", "bn_stats", "bn_aggr", "reciprocal", "stream_shuffle",
    "iota", "affine_select", "activation", "square", "sqrt", "mul", "add",
    "copy", "sign", "affine_then_add", "affine_mul_reduce",
    "tensor_tensor_reduce", "max", "max_index", "pool", "pool_avg", "pool_max",
})
_CHAIN_SEM = {"vector": "dv", "scalar": "ac", "gpsimd": "po"}


class _MockIns:
    def then_inc(self, *a, **k):
        return self

    def annotate(self, *a, **k):
        return self


class _CountMock:

    def __init__(self):
        self.n = 0

    def __getattr__(self, name):
        if name in _CHAINED_OPS:
            def f(*a, **k):
                self.n += 1
                return _MockIns()
        else:
            def f(*a, **k):
                return _MockIns()
        return f


class _ChainProxy:

    def __init__(self, eng, sem, state):
        self._eng = eng
        self._sem = sem
        self._st = state

    def __getattr__(self, name):
        attr = getattr(self._eng, name)
        if name in _CHAINED_OPS:
            def f(*a, **k):
                if self._st["n"] > 0:
                    self._eng.wait_ge(self._sem, self._st["n"])
                ins = attr(*a, **k)
                ins.then_inc(self._sem, 1)
                self._st["n"] += 1
                return ins
            return f
        return attr


class Prog:

    ENGINES = ("sync", "vector", "scalar", "gpsimd", "tensor")

    def __init__(self, nc):
        self.nc = nc
        self.stack = ExitStack()
        self.items = {e: [] for e in self.ENGINES}
        self.semh = {}
        self.semc = {}
        self._uid = 0

    def uid(self):
        self._uid += 1
        return self._uid

    def sem(self, name):
        if name not in self.semh:
            self.semh[name] = self.stack.enter_context(self.nc.semaphore(name))
            self.semc[name] = 0
        return self.semh[name]

    def sbuf(self, name, shape, dt=F32):
        return self.stack.enter_context(self.nc.sbuf_tensor(name, list(shape), dt))

    def psum(self, name, shape, dt=F32):
        return self.stack.enter_context(self.nc.psum_tensor(name, list(shape), dt))

    def emit(self, eng, fn, waits=(), inc=None, inc_n=1):
        for s, _ in waits:
            self.sem(s)
        chain = _CHAIN_SEM.get(eng)
        if chain is not None:
            assert inc is None or inc == chain, f"use chain sem {chain} on {eng}"
            mock = _CountMock()
            fn(mock)
            self.sem(chain)
            self.semc[chain] += mock.n
            self.items[eng].append((tuple(waits), fn, None, 0))
            return self.semc[chain]
        if inc is not None:
            self.sem(inc)
            self.semc[inc] += inc_n
        self.items[eng].append((tuple(waits), fn, inc, inc_n))
        return self.semc[inc] if inc is not None else None

    def c(self, sem):
        self.sem(sem)
        return self.semc[sem]

    def build(self):
        chain_state = {e: {"n": 0} for e in _CHAIN_SEM}
        with self.nc.Block() as block:
            def mk(eng_name):
                def runner(eng):
                    chain = _CHAIN_SEM.get(eng_name)
                    if chain is not None:
                        eng_x = _ChainProxy(eng, self.semh[chain], chain_state[eng_name])
                    else:
                        eng_x = eng
                    for waits, fn, inc, inc_n in self.items[eng_name]:
                        for s, v in waits:
                            eng.wait_ge(self.semh[s], v)
                        ins = fn(eng_x)
                        if inc is not None:
                            assert ins is not None, f"emit with inc={inc} returned None"
                            ins.then_inc(self.semh[inc], inc_n)
                return runner
            for e in self.ENGINES:
                if self.items[e]:
                    getattr(block, e)(mk(e))
        for e, st in chain_state.items():
            assert st["n"] == self.semc[_CHAIN_SEM[e]], \
                f"chain count mismatch on {e}: {st['n']} != {self.semc[_CHAIN_SEM[e]]}"
        return self.nc


def common_consts(p):
    cst = {}
    cst["ones_col_f"] = p.sbuf("c_ones_col_f", (128, 1), F32)
    cst["ones_row_f"] = p.sbuf("c_ones_row_f", (1, 128), F32)
    cst["ones_row_b"] = p.sbuf("c_ones_row_b", (1, 128), BF16)
    cst["ones_tb_b"] = p.sbuf("c_ones_tb_b", (1, TB), BF16)
    cst["eps_row"] = p.sbuf("c_eps_row", (1, 1), F32)
    cst["id_f"] = p.sbuf("c_id_f", (128, 128), F32)
    cst["id_b"] = p.sbuf("c_id_b", (128, 128), BF16)

    def _(v):
        v.memset(cst["ones_col_f"][:, :], 1.0)
        v.memset(cst["ones_row_f"][:, :], 1.0)
        v.memset(cst["ones_row_b"][:, :], 1.0)
        v.memset(cst["eps_row"][:, :], LN_EPS)
        return v.memset(cst["ones_tb_b"][:, :], 1.0)
    p.emit("vector", _, inc="dv")

    def _(g):
        g.memset(cst["id_f"][:, :], 1.0)
        g.memset(cst["id_b"][:, :], 1.0)
        g.affine_select(cst["id_f"][:, :], cst["id_f"][:, :], [[-1, 128]], ALU.is_equal,
                        0.0, base=0, channel_multiplier=1)
        return g.affine_select(cst["id_b"][:, :], cst["id_b"][:, :], [[-1, 128]], ALU.is_equal,
                               0.0, base=0, channel_multiplier=1)
    p.emit("gpsimd", _, inc="po")
    cst["ready"] = [("dv", p.c("dv")), ("po", p.c("po"))]
    return cst


def ln_pool(p, ext=None, tag=""):
    ent = (ext.enter_context if ext is not None
           else p.stack.enter_context)
    nc = p.nc
    return {
        "sq": ent(nc.sbuf_tensor(f"lnp_sq{tag}", [128, TB], F32)),
        "srow": ent(nc.sbuf_tensor(f"lnp_sum{tag}", [1, T], F32)),
        "qrow": ent(nc.sbuf_tensor(f"lnp_qsum{tag}", [1, T], F32)),
        "irow": ent(nc.sbuf_tensor(f"lnp_istd{tag}", [1, T], F32)),
    }


def emit_ln_fm(p, cst, pp, lnp, x_tiles, out_tiles, wait_x, gamma_col=None, beta_col=None):
    sq, srow, qrow = lnp["sq"], lnp["srow"], lnp["qrow"]
    mrow, irow = lnp["srow"], lnp["irow"]
    psA, psB = pp["rowA"], pp["rowB"]

    for tb in range(NTB):
        t0, t1 = tb * TB, (tb + 1) * TB
        dv_now = p.c("dv")

        def _(t, t0=t0, t1=t1):
            last = None
            for dc in range(DC):
                last = t.matmul(psA[0:1, :], cst["ones_col_f"][:, :],
                                x_tiles[dc][:, t0:t1], start=(dc == 0), stop=(dc == DC - 1))
            return last
        pe_v = p.emit("tensor", _, waits=list(wait_x) + cst["ready"] + [("dv", dv_now)],
                      inc="pe")

        def _(v, t0=t0, t1=t1):
            return v.tensor_copy(srow[:, t0:t1], psA[0:1, :])
        p.emit("vector", _, waits=[("pe", pe_v)], inc="dv")

        for dc in range(DC):
            def _(s, dc=dc, t0=t0, t1=t1):
                return s.activation(sq[:, :], x_tiles[dc][:, t0:t1], AF.Square)
            ac_v = p.emit("scalar", _, waits=list(wait_x) + [("pe", p.c("pe"))], inc="ac")

            def _(t, dc=dc):
                return t.matmul(psB[0:1, :], cst["ones_col_f"][:, :], sq[:, :],
                                start=(dc == 0), stop=(dc == DC - 1))
            p.emit("tensor", _, waits=[("ac", ac_v), ("dv", p.c("dv"))], inc="pe")

        def _(v, t0=t0, t1=t1):
            return v.tensor_copy(qrow[:, t0:t1], psB[0:1, :])
        p.emit("vector", _, waits=[("pe", p.c("pe"))], inc="dv")

    def _(v):
        v.tensor_scalar_mul(mrow[:, :], srow[:, :], 1.0 / D)
        v.tensor_scalar_mul(qrow[:, :], qrow[:, :], 1.0 / D)
        v.tensor_mul(irow[:, :], mrow[:, :], mrow[:, :])
        return v.tensor_sub(qrow[:, :], qrow[:, :], irow[:, :])
    dv_var = p.emit("vector", _, waits=[("dv", p.c("dv"))], inc="dv")

    def _(s):
        return s.activation(qrow[:, :], qrow[:, :], AF.Sqrt, bias=cst["eps_row"][:, :])
    ac_std = p.emit("scalar", _, waits=[("dv", dv_var)], inc="ac")

    def _(v):
        v.reciprocal(irow[:, :], qrow[:, :])
        return v.tensor_mul(mrow[:, :], mrow[:, :], irow[:, :])
    dv_ist = p.emit("vector", _, waits=[("ac", ac_std)], inc="dv")

    for tb in range(NTB):
        t0, t1 = tb * TB, (tb + 1) * TB
        dv_prev = p.c("dv")

        def _(t, t0=t0, t1=t1):
            t.matmul(pp["bc0"][:, :], cst["ones_row_f"][:, :], irow[0:1, t0:t1],
                     start=True, stop=True)
            return t.matmul(pp["bc1"][:, :], cst["ones_row_f"][:, :], mrow[0:1, t0:t1],
                            start=True, stop=True)
        pe_bc = p.emit("tensor", _, waits=[("dv", dv_prev), ("dv", dv_ist)], inc="pe")

        for dc in range(DC):
            if gamma_col is None:
                def _(v, dc=dc, t0=t0, t1=t1):
                    v.tensor_mul(sq[:, :], x_tiles[dc][:, t0:t1], pp["bc0"][:, :])
                    return v.tensor_sub(out_tiles[dc][:, t0:t1], sq[:, :], pp["bc1"][:, :])
            else:
                def _(v, dc=dc, t0=t0, t1=t1):
                    v.tensor_mul(sq[:, :], x_tiles[dc][:, t0:t1], pp["bc0"][:, :])
                    v.tensor_sub(sq[:, :], sq[:, :], pp["bc1"][:, :])
                    return v.tensor_scalar(out_tiles[dc][:, t0:t1], sq[:, :],
                                           gamma_col[dc], beta_col[dc], ALU.mult, ALU.add)
            p.emit("vector", _, waits=[("pe", pe_bc)], inc="dv")
    return [("dv", p.c("dv"))]


def emit_top2(p, cst, pp, xn_tiles, wt_router, brow_router, rw_tm, wait_xn, n_exp=8):
    u = p.uid()
    pr = p.sbuf(f"rt_pr{u}", (128, n_exp), F32)
    tmp = p.sbuf(f"rt_tmp{u}", (128, n_exp), F32)
    cum = p.sbuf(f"rt_cum{u}", (128, n_exp), F32)
    sel = p.sbuf(f"rt_sel{u}", (128, n_exp), F32)
    sel2 = p.sbuf(f"rt_sel2{u}", (128, n_exp), F32)
    zer = p.sbuf(f"rt_zero{u}", (128, n_exp), F32)
    col = p.sbuf(f"rt_col{u}", (128, 1), F32)
    csum = p.sbuf(f"rt_csum{u}", (128, 1), F32)
    pst = pp["mm0"]

    def _(v):
        return v.memset(zer[:, :], 0.0)
    p.emit("vector", _, inc="dv")

    for tc in range(T // 128):
        c0 = tc * 128
        def _(t, c0=c0):
            last = None
            for dc in range(DC):
                last = t.matmul(pst[:, 0:n_exp], xn_tiles[dc][:, c0:c0 + 128],
                                wt_router[dc][:, :], start=(dc == 0), stop=False)
            if brow_router is not None:
                last = t.matmul(pst[:, 0:n_exp], cst["ones_row_b"][:, :],
                                brow_router[0:1, :], start=False, stop=True)
            else:
                last.stop = True
            return last
        pe_l = p.emit("tensor", _, waits=list(wait_xn) + [("dv", p.c("dv"))], inc="pe")

        def _(v):
            return v.reduce_max(col[:, :], pst[:, 0:n_exp], AX.X, negate=True)
        dv_m = p.emit("vector", _, waits=[("pe", pe_l)], inc="dv")

        def _(s):
            return s.activation(pr[:, :], pst[:, 0:n_exp], AF.Exp, bias=col[:, :],
                                accum_out=csum[:, :])
        ac_e = p.emit("scalar", _, waits=[("dv", dv_m)], inc="ac")

        def _(v, tc=tc):
            v.reciprocal(col[:, :], csum[:, :])
            v.tensor_scalar(pr[:, :], pr[:, :], col[:, :], None, ALU.mult)
            v.reduce_max(col[:, :], pr[:, :], AX.X)
            v.tensor_scalar(tmp[:, :], pr[:, :], col[:, :], None, ALU.is_ge)
            v.tensor_tensor_scan(cum[:, :], tmp[:, :], zer[:, :], 0.0, ALU.add, ALU.add)
            v.tensor_scalar(cum[:, :], cum[:, :], 1.0, None, ALU.is_le)
            v.tensor_mul(sel[:, :], tmp[:, :], cum[:, :])
            v.scalar_tensor_tensor(tmp[:, :], sel[:, :], -1e30, pr[:, :], ALU.mult, ALU.add)
            v.reduce_max(col[:, :], tmp[:, :], AX.X)
            v.tensor_scalar(sel2[:, :], tmp[:, :], col[:, :], None, ALU.is_ge)
            v.tensor_tensor_scan(cum[:, :], sel2[:, :], zer[:, :], 0.0, ALU.add, ALU.add)
            v.tensor_scalar(cum[:, :], cum[:, :], 1.0, None, ALU.is_le)
            v.tensor_mul(sel2[:, :], sel2[:, :], cum[:, :])
            v.tensor_add(sel[:, :], sel[:, :], sel2[:, :])
            v.tensor_mul(pr[:, :], pr[:, :], sel[:, :])
            return v.tensor_copy(rw_tm[:, tc::T // 128], pr[:, :])
        dv_rw = p.emit("vector", _, waits=[("ac", ac_e)], inc="dv")
    return [("dv", p.c("dv"))]


def emit_rw_row(p, cst, pp, rw_tm, rw_row, e, wait_rw, n_exp=8):
    NTC = T // 128
    for tc in range(NTC):
        def _(t, e=e, tc=tc):
            return t.transpose(pp["tp0"][0:1, 0:128], rw_tm[:, e * NTC + tc:e * NTC + tc + 1],
                               cst["id_f"][:, :])
        pe_t = p.emit("tensor", _, waits=list(wait_rw) + [("dv", p.c("dv"))], inc="pe")

        def _(v, tc=tc):
            return v.tensor_copy(rw_row[0:1, tc * 128:(tc + 1) * 128], pp["tp0"][0:1, 0:128])
        p.emit("vector", _, waits=[("pe", pe_t)], inc="dv")
    return [("dv", p.c("dv"))]


def dma_batch(p, eng, pairs, sem, waits=()):
    p.sem(sem)
    n = len(pairs)
    semh = p.semh[sem]

    def fn(eng_obj, pairs=tuple(pairs)):
        last = None
        for out_ap, in_ap in pairs:
            last = eng_obj.dma_start(out_ap, in_ap).then_inc(semh, 16)
        return None
    for s_, _ in waits:
        p.sem(s_)
    p.semc[sem] += 16 * n
    p.items[eng].append((tuple(waits), fn, None, 0))
    return p.semc[sem]


def emit_transpose_out(p, cst, pp, x_tiles, d_out, s, stage, wait_x=()):
    for tc in range(T // 128):
        sb = stage[tc % 2]
        for dc in range(DC):
            tp = pp["tp0"] if dc % 2 == 0 else pp["tp1"]

            def _(t, tc=tc, dc=dc, tp=tp):
                return t.transpose(tp[:, :], x_tiles[dc][:, tc * 128:(tc + 1) * 128],
                                   cst["id_f"][:, :])
            pe_t = p.emit("tensor", _, waits=list(wait_x) + [("dv", p.c("dv")),
                                                             ("st", p.c("st"))], inc="pe")

            def _(v, dc=dc, sb=sb, tp=tp):
                return v.tensor_copy(sb[:, dc * 128:(dc + 1) * 128], tp[:, :])
            p.emit("vector", _, waits=[("pe", pe_t)], inc="dv")
        dma_batch(p, "sync", [(d_out[s, tc * 128:(tc + 1) * 128, :], sb[:, :])],
                  "st", waits=[("dv", p.c("dv"))])


def build_l1ffn_launch():
    nc = bass.Bass("TRN2", target_bir_lowering=False)
    NE = 8
    d_x3 = nc.dram_tensor("x3", [NSEQ, D, T], F32, kind="ExternalInput")
    d_wet = nc.dram_tensor("WeT", [NE, D, D], F32, kind="ExternalInput")
    d_be = nc.dram_tensor("be", [NE, D], F32, kind="ExternalInput")
    d_wrt = nc.dram_tensor("WrT", [D, NE], F32, kind="ExternalInput")
    d_br = nc.dram_tensor("br", [1, NE], F32, kind="ExternalInput")
    d_x4 = nc.dram_tensor("x4", [NSEQ, T, D], F32, kind="ExternalOutput")

    p = Prog(nc)
    with p.stack:
        cst = common_consts(p)
        pp = {
            "rowA": p.psum("pp_rowA", (1, TB)), "rowB": p.psum("pp_rowB", (1, TB)),
            "bc0": p.psum("pp_bc0", (128, TB)), "bc1": p.psum("pp_bc1", (128, TB)),
            "mm0": p.psum("pp_mm0", (128, TB)), "mm1": p.psum("pp_mm1", (128, TB)),
            "tp0": p.psum("pp_tp0", (128, 128)), "tp1": p.psum("pp_tp1", (128, 128)),
        }
        lnp = ln_pool(p)
        x_tiles = [p.sbuf(f"x_{dc}", (128, T)) for dc in range(DC)]
        xn = [p.sbuf(f"xn_{dc}", (128, T), BF16) for dc in range(DC)]
        wrt = [p.sbuf(f"wrt_{dc}", (128, NE), BF16) for dc in range(DC)]
        br_row = p.sbuf("br_row", (1, NE), BF16)
        be_row2 = [p.sbuf(f"be_{b}", (1, D), BF16) for b in range(2)]
        rw_tm = p.sbuf("rw_tm", (128, NE * (T // 128)))
        rw_row = p.sbuf("rw_row", (1, T))
        rwbc = p.sbuf("rwbc", (128, T))
        wt = [[p.sbuf(f"wt{b}_{dc}", (128, D), BF16) for dc in range(DC)] for b in range(2)]
        stage = [p.sbuf(f"stage{b}", (128, D)) for b in range(2)]
        evtmp = p.sbuf("evtmp", (128, TB))

        gl0 = dma_batch(p, "gpsimd",
                        [(wrt[dc][:, :], d_wrt[dc * 128:(dc + 1) * 128, :])
                         for dc in range(DC)]
                        + [(br_row[:, :], d_br[:, :])],
                        "gsm")

        pe_after_expert = {}
        for s in range(NSEQ):
            ld_v = dma_batch(p, "sync",
                             [(x_tiles[dc][:, :], d_x3[s, dc * 128:(dc + 1) * 128, :])
                              for dc in range(DC)],
                             "ld", waits=[("dv", p.c("dv")), ("pe", p.c("pe")),
                                          ("st", p.c("st"))])
            wait_x = [("ld", ld_v)]

            wait_xn = emit_ln_fm(p, cst, pp, lnp, x_tiles, xn, wait_x)
            wait_rw = emit_top2(p, cst, pp, xn, wrt, br_row, rw_tm,
                                wait_xn + [("gsm", gl0)], n_exp=NE)

            for e in range(NE):
                buf = wt[e % 2]
                k = (s, e)
                prev2 = pe_after_expert.get((s, e - 2),
                                            pe_after_expert.get((s - 1, NE - 2 + e), 0))
                be_row = be_row2[e % 2]
                gl = dma_batch(p, "gpsimd",
                               [(buf[dc][:, :], d_wet[e, dc * 128:(dc + 1) * 128, :])
                                for dc in range(DC)]
                               + [(be_row[:, :], d_be[e:e + 1, :])],
                               f"gw{e % 2}", waits=[("pe", prev2)])
                wait_row = emit_rw_row(p, cst, pp, rw_tm, rw_row, e, wait_rw, n_exp=NE)
                for tb in range(NTB):
                    t0, t1 = tb * TB, (tb + 1) * TB

                    def _(t, t0=t0, t1=t1):
                        return t.matmul(pp["bc0"][:, :], cst["ones_row_f"][:, :],
                                        rw_row[0:1, t0:t1], start=True, stop=True)
                    pe_b = p.emit("tensor", _,
                                  waits=list(wait_row) + [("dv", p.c("dv"))], inc="pe")

                    def _(v, t0=t0, t1=t1):
                        return v.tensor_copy(rwbc[:, t0:t1], pp["bc0"][:, :])
                    p.emit("vector", _, waits=[("pe", pe_b)], inc="dv")
                dv_rwbc = p.c("dv")

                for oc in range(DC):
                    for tb in range(NTB):
                        t0, t1 = tb * TB, (tb + 1) * TB
                        ps = pp["mm0"] if (oc * NTB + tb) % 2 == 0 else pp["mm1"]

                        def mmfn(t, oc=oc, t0=t0, t1=t1, ps=ps, buf=buf, be_row=be_row):
                            t.matmul(ps[:, :], be_row[0:1, oc * 128:(oc + 1) * 128],
                                     cst["ones_tb_b"][:, :], start=True, stop=False)
                            last = None
                            for dc in range(DC):
                                last = t.matmul(ps[:, :], buf[dc][:, oc * 128:(oc + 1) * 128],
                                                xn[dc][:, t0:t1],
                                                start=False, stop=(dc == DC - 1))
                            return last
                        pe_m = p.emit("tensor", mmfn,
                                      waits=[(f"gw{e % 2}", gl), ("dv", p.c("dv")),
                                             ("ac", p.c("ac")), ("dv", dv_rwbc)], inc="pe")

                        def evfn(v, oc=oc, t0=t0, t1=t1, ps=ps):
                            v.tensor_mul(evtmp[:, :], ps[:, :], rwbc[:, t0:t1])
                            return v.tensor_add(x_tiles[oc][:, t0:t1],
                                                x_tiles[oc][:, t0:t1], evtmp[:, :])
                        p.emit("vector", evfn, waits=[("pe", pe_m)], inc="dv")
                pe_after_expert[k] = p.c("pe")

            emit_transpose_out(p, cst, pp, x_tiles, d_x4, s, stage)

        p.emit("sync", lambda y: None, waits=[("st", p.c("st"))])
        p.build()
    return nc


HID = 2816
HHALF = HID // 2
MCH = HHALF // 128


def build_l0ffn_launch():
    nc = bass.Bass("TRN2", target_bir_lowering=False)
    NE = 8
    d_x1 = nc.dram_tensor("x1", [NSEQ, D, T], F32, kind="ExternalInput")
    d_gwt = nc.dram_tensor("gwT", [D, NE], F32, kind="ExternalInput")
    d_w1 = nc.dram_tensor("w1", [NE, D, HID], F32, kind="ExternalInput")
    d_w2 = nc.dram_tensor("w2", [NE, HID, D], F32, kind="ExternalInput")
    d_w3 = nc.dram_tensor("w3", [NE, D, HID], F32, kind="ExternalInput")
    d_x2 = nc.dram_tensor("x2", [NSEQ, D, T], F32, kind="ExternalOutput")

    p = Prog(nc)
    with p.stack:
        cst = common_consts(p)
        pp = {
            "rowA": p.psum("pp_rowA", (1, TB)), "rowB": p.psum("pp_rowB", (1, TB)),
            "bc0": p.psum("pp_bc0", (128, TB)), "bc1": p.psum("pp_bc1", (128, TB)),
            "mm0": p.psum("pp_mm0", (128, TB)), "mm1": p.psum("pp_mm1", (128, TB)),
            "tp0": p.psum("pp_tp0", (128, 128)), "tp1": p.psum("pp_tp1", (128, 128)),
        }
        x_tiles = [p.sbuf(f"x_{dc}", (128, T)) for dc in range(DC)]
        xn = [p.sbuf(f"xn_{dc}", (128, T), BF16) for dc in range(DC)]
        gwt = [p.sbuf(f"gwt_{dc}", (128, NE), BF16) for dc in range(DC)]
        zrow = p.sbuf("zrow", (1, NE), BF16)
        rw_tm = p.sbuf("rw_tm", (128, NE * (T // 128)))
        w1b = [p.sbuf(f"w1b_{dc}", (128, HHALF), BF16) for dc in range(DC)]
        w3b = [p.sbuf(f"w3b_{dc}", (128, HHALF), BF16) for dc in range(DC)]
        w2b = [p.sbuf(f"w2b_{k}", (128, D), BF16) for k in range(MCH)]
        TBF, NTBF = 256, T // 256
        uscr = p.sbuf("uscr", (128, 6656), F32)
        lnp = {"sq": uscr[:, 0:512], "srow": uscr[0:1, 512:2560],
               "qrow": uscr[0:1, 2560:4608], "irow": uscr[0:1, 4608:6656]}
        rw_row = uscr[:, 0:1024].bitcast(BF16)[0:1, 0:T]
        rwbc = uscr[:, 1024:2048].bitcast(BF16)
        xe = [uscr[:, 2048 + dc * 128:2048 + (dc + 1) * 128].bitcast(BF16)
              for dc in range(DC)]
        hb = uscr[:, 3072:4480].bitcast(BF16)
        hs = uscr[:, 4480:4736]

        def _(v):
            return v.memset(zrow[:, :], 0.0)
        p.emit("vector", _)

        gl0 = dma_batch(p, "gpsimd",
                        [(gwt[dc][:, :], d_gwt[dc * 128:(dc + 1) * 128, :])
                         for dc in range(DC)], "gsm")

        for s in range(NSEQ):
            ld_v = dma_batch(p, "sync",
                             [(x_tiles[dc][:, :], d_x1[s, dc * 128:(dc + 1) * 128, :])
                              for dc in range(DC)],
                             "ld", waits=[("dv", p.c("dv")), ("pe", p.c("pe")),
                                          ("st", p.c("st"))])
            wait_x = [("ld", ld_v)]

            wait_xn = emit_ln_fm(p, cst, pp, lnp, x_tiles, xn, wait_x)
            wait_rw = emit_top2(p, cst, pp, xn, gwt, zrow, rw_tm,
                                wait_xn + [("gsm", gl0)], n_exp=NE)

            for e in range(NE):
                wait_row = emit_rw_row(p, cst, pp, rw_tm, rw_row, e, wait_rw, n_exp=NE)
                for tb in range(NTB):
                    t0, t1 = tb * TB, (tb + 1) * TB

                    def _(t, t0=t0, t1=t1):
                        return t.matmul(pp["bc0"][:, :], cst["ones_row_b"][:, :],
                                        rw_row[0:1, t0:t1], start=True, stop=True)
                    pe_b = p.emit("tensor", _,
                                  waits=list(wait_row) + [("dv", p.c("dv"))], inc="pe")

                    def _(v, t0=t0, t1=t1):
                        return v.tensor_copy(rwbc[:, t0:t1], pp["bc0"][:, :])
                    p.emit("vector", _, waits=[("pe", pe_b)], inc="dv")
                dv_rwbc = p.c("dv")

                for half in range(2):
                    h0 = half * HHALF
                    gw = dma_batch(
                        p, "gpsimd",
                        [(w1b[dc][:, :], d_w1[e, dc * 128:(dc + 1) * 128, h0:h0 + HHALF])
                         for dc in range(DC)]
                        + [(w3b[dc][:, :], d_w3[e, dc * 128:(dc + 1) * 128, h0:h0 + HHALF])
                           for dc in range(DC)]
                        + [(w2b[k][:, :], d_w2[e, h0 + k * 128:h0 + (k + 1) * 128, :])
                           for k in range(MCH)],
                        "gwl", waits=[("pe", p.c("pe"))])

                    for tb in range(NTBF):
                        t0, t1 = tb * TBF, (tb + 1) * TBF
                        def _(v, t0=t0, t1=t1):
                            last = None
                            for dc in range(DC):
                                last = v.tensor_mul(xe[dc][:, :], xn[dc][:, t0:t1],
                                                    rwbc[:, t0:t1])
                            return last
                        dv_xe = p.emit("vector", _, waits=[("dv", dv_rwbc),
                                                           ("pe", p.c("pe"))])

                        for mc in range(MCH):
                            m0 = mc * 128

                            def _(t, m0=m0):
                                for dc in range(DC):
                                    t.matmul(pp["mm0"][:, 0:TBF], w1b[dc][:, m0:m0 + 128],
                                             xe[dc][:, :], start=(dc == 0),
                                             stop=(dc == DC - 1))
                                last = None
                                for dc in range(DC):
                                    last = t.matmul(pp["mm1"][:, 0:TBF], w3b[dc][:, m0:m0 + 128],
                                                    xe[dc][:, :], start=(dc == 0),
                                                    stop=(dc == DC - 1))
                                return last
                            pe_h = p.emit("tensor", _,
                                          waits=[("gwl", gw), ("dv", dv_xe),
                                                 ("ac", p.c("ac")), ("dv", p.c("dv"))],
                                          inc="pe")

                            def _(s_, mc=mc):
                                return s_.activation(hs[:, :], pp["mm0"][:, 0:TBF], AF.Silu)
                            ac_s = p.emit("scalar", _, waits=[("pe", pe_h)])

                            def _(v, mc=mc):
                                return v.tensor_mul(hb[:, mc * TBF:(mc + 1) * TBF],
                                                    hs[:, :], pp["mm1"][:, 0:TBF])
                            p.emit("vector", _, waits=[("ac", ac_s), ("pe", pe_h)])
                        dv_h = p.c("dv")

                        for oc in range(DC):
                            ps = pp["mm0"] if oc % 2 == 0 else pp["mm1"]

                            def _(t, oc=oc, ps=ps):
                                last = None
                                for k in range(MCH):
                                    last = t.matmul(ps[:, 0:TBF], w2b[k][:, oc * 128:(oc + 1) * 128],
                                                    hb[:, k * TBF:(k + 1) * TBF],
                                                    start=(k == 0), stop=(k == MCH - 1))
                                return last
                            pe_y = p.emit("tensor", _,
                                          waits=[("dv", dv_h), ("dv", p.c("dv")),
                                                 ("ac", p.c("ac"))], inc="pe")

                            def _(v, oc=oc, t0=t0, t1=t1, ps=ps):
                                return v.tensor_add(x_tiles[oc][:, t0:t1],
                                                    x_tiles[oc][:, t0:t1], ps[:, 0:TBF])
                            p.emit("vector", _, waits=[("pe", pe_y)])

            dma_batch(p, "sync",
                      [(d_x2[s, dc * 128:(dc + 1) * 128, :], x_tiles[dc][:, :])
                       for dc in range(DC)],
                      "st", waits=[("dv", p.c("dv"))])

        p.emit("sync", lambda y: None, waits=[("st", p.c("st"))])
        p.build()
    return nc


def build_l0attn_launch():
    nc = bass.Bass("TRN2", target_bir_lowering=False)
    NH, HD = 16, 64
    d_x = nc.dram_tensor("x", [NSEQ, T, D], F32, kind="ExternalInput")
    d_wqkv = nc.dram_tensor("wqkv", [D, 3 * D], F32, kind="ExternalInput")
    d_bq = nc.dram_tensor("bq", [1, 3 * D], F32, kind="ExternalInput")
    d_wo = nc.dram_tensor("wo", [D, D], F32, kind="ExternalInput")
    d_bo = nc.dram_tensor("bo", [1, D], F32, kind="ExternalInput")
    d_xfm = nc.dram_tensor("xfm_scr", [NSEQ, D, T], F32)
    d_x1 = nc.dram_tensor("x1", [NSEQ, D, T], F32, kind="ExternalOutput")

    p = Prog(nc)
    with p.stack:
        cst = common_consts(p)
        pp = {
            "rowA": p.psum("pp_rowA", (1, TB)), "rowB": p.psum("pp_rowB", (1, TB)),
            "bc0": p.psum("pp_bc0", (128, TB)), "bc1": p.psum("pp_bc1", (128, TB)),
            "mm0": p.psum("pp_mm0", (128, TB)), "mm1": p.psum("pp_mm1", (128, TB)),
        }
        tpb = [p.psum(f"pp_tpb{i}", (128, 128), BF16) for i in range(2)]
        po_ = pp["mm1"][:, 0:128]

        uscrA = p.sbuf("uscrA", (128, 16384))
        xfm_t = [uscrA[:, dc * 2048:(dc + 1) * 2048] for dc in range(DC)]
        V_tm = uscrA[:, 0:8192].bitcast(BF16)
        O_fm = [uscrA[:, 8192 + dc * 1024:8192 + (dc + 1) * 1024].bitcast(BF16)
                for dc in range(DC)]
        uscrB = p.sbuf("uscrB", (128, 6656))
        x_tm = [uscrB[:, 0:1024], uscrB[:, 1024:2048]]
        lnp = {"sq": uscrB[:, 0:512], "srow": uscrB[0:1, 512:2560],
               "qrow": uscrB[0:1, 2560:4608], "irow": uscrB[0:1, 4608:6656]}
        score_sb = uscrB[:, 0:2048]
        dot2 = uscrB[:, 2048:2560]
        rcp = uscrB[:, 2560:3072]
        Pb = uscrB[:, 3072:4096].bitcast(BF16)
        qnb = uscrB[:, 4608:5632].bitcast(BF16)[0:1, 0:T]
        knb = uscrB[:, 5632:6656].bitcast(BF16)[0:1, 0:T]
        uscrC = p.sbuf("uscrC", (128, 4096))
        wv = [uscrC[:, dc * 512:(dc + 1) * 512].bitcast(BF16) for dc in range(DC)]
        PT = uscrC[:, 0:1088].bitcast(BF16)
        Qsq = uscrC[0:64, 1088:3136]

        xn = [p.sbuf(f"xn_{dc}", (128, T), BF16) for dc in range(DC)]
        QA = p.sbuf("QA", (64, T), BF16); QB = p.sbuf("QB", (64, T), BF16)
        MA = p.sbuf("MA", (64, T), BF16); MB = p.sbuf("MB", (64, T), BF16)
        KA = p.sbuf("KA", (64, T), BF16); KB = p.sbuf("KB", (64, T), BF16)
        wqs = [p.sbuf(f"wqs{b}", (128, DC * 128), BF16) for b in range(2)]
        wks = [p.sbuf(f"wks{b}", (128, DC * 128), BF16) for b in range(2)]
        wo = [p.sbuf(f"wo_{dc}", (128, D), BF16) for dc in range(DC)]
        bqv_row = p.sbuf("bqv_row", (1, D), BF16)
        bq_cols = p.sbuf("bq_cols", (128, 2 * NH))
        bo_row = p.sbuf("bo_row", (1, D), BF16)
        mcol = p.sbuf("mcol", (128, 1))
        lcol = p.sbuf("lcol", (128, 1))
        b2col = p.sbuf("b2col", (128, 1))
        xchunk = [p.sbuf(f"xch{b}", (128, TB)) for b in range(2)]
        obuf = [p.sbuf(f"obuf{b}", (128, TB)) for b in range(2)]

        maskreg = {}

        def _(g):
            import struct
            r = g.alloc_register("maskfill")
            g.reg_mov(r, struct.unpack('<i', struct.pack('<f', -1e9))[0])
            maskreg["r"] = r
            return None
        p.emit("gpsimd", _)

        gsm = dma_batch(p, "gpsimd",
                        [(bqv_row[:, :], d_bq[0:1, 2 * D:3 * D]),
                         (bo_row[:, :], d_bo[0:1, :])]
                        + [(bq_cols[:, h:h + 1],
                            d_bq[0:1, h * 128:h * 128 + 128].rearrange("o i -> i o"))
                           for h in range(8)]
                        + [(bq_cols[:, 16 + h:17 + h],
                            d_bq[0:1, D + h * 128:D + h * 128 + 128].rearrange("o i -> i o"))
                           for h in range(8)]
                        + [(wo[dc][:, :], d_wo[dc * 128:(dc + 1) * 128, :])
                           for dc in range(DC)],
                        "gsm")

        for s in range(NSEQ):
            pe_pre = p.c("pe")
            dv_pre = p.c("dv")
            for tc in range(T // 128):
                ld_v = dma_batch(p, "sync",
                                 [(x_tm[tc % 2][:, :], d_x[s, tc * 128:(tc + 1) * 128, :])],
                                 "ld", waits=[("pe", p.c("pe")), ("dv", p.c("dv"))])
                for dc in range(DC):
                    tp = pp["bc0"][:, 0:128] if dc % 2 == 0 else pp["bc1"][:, 0:128]

                    def _(t, tc=tc, dc=dc, tp=tp):
                        return t.transpose(tp, x_tm[tc % 2][:, dc * 128:(dc + 1) * 128],
                                           cst["id_f"][:, :])
                    pe_t = p.emit("tensor", _, waits=[("ld", ld_v), ("dv", p.c("dv"))]
                                  + cst["ready"], inc="pe")

                    def _(v, tc=tc, dc=dc, tp=tp):
                        return v.tensor_copy(xfm_t[dc][:, tc * 128:(tc + 1) * 128], tp)
                    p.emit("vector", _, waits=[("pe", pe_t)])
            sx = dma_batch(p, "sync",
                           [(d_xfm[s, dc * 128:(dc + 1) * 128, :], xfm_t[dc][:, :])
                            for dc in range(DC)],
                           "sx", waits=[("dv", p.c("dv"))])
            wait_xn = emit_ln_fm(p, cst, pp, lnp, xfm_t, xn, [("dv", p.c("dv"))])

            gv = dma_batch(p, "gpsimd",
                           [(wv[dc][:, :], d_wqkv[dc * 128:(dc + 1) * 128, 2 * D:3 * D])
                            for dc in range(DC)],
                           "gv", waits=[("pe", p.c("pe")), ("dv", p.c("dv")),
                                        ("sx", sx)])
            for tc in range(T // 128):
                for vh in range(2):
                    ps = pp["mm0"] if vh == 0 else pp["mm1"]

                    def _(t, tc=tc, vh=vh, ps=ps):
                        t.matmul(ps[:, :], cst["ones_row_b"][:, :],
                                 bqv_row[0:1, vh * TB:(vh + 1) * TB], start=True, stop=False)
                        last = None
                        for dc in range(DC):
                            last = t.matmul(ps[:, :], xn[dc][:, tc * 128:(tc + 1) * 128],
                                            wv[dc][:, vh * TB:(vh + 1) * TB],
                                            start=False, stop=(dc == DC - 1))
                        return last
                    pe_v = p.emit("tensor", _,
                                  waits=list(wait_xn) + [("gv", gv), ("gsm", gsm),
                                                         ("dv", p.c("dv"))], inc="pe")

                    def _(v, tc=tc, vh=vh, ps=ps):
                        return v.tensor_copy(V_tm[:, tc * 1024 + vh * TB:tc * 1024 + (vh + 1) * TB],
                                             ps[:, :])
                    p.emit("vector", _, waits=[("pe", pe_v)])
            dv_vdone = p.c("dv")

            for hp in range(NH // 2):
                par = hp % 2
                gq = dma_batch(
                    p, "gpsimd",
                    [(wqs[par][:, dc * 128:(dc + 1) * 128],
                      d_wqkv[dc * 128:(dc + 1) * 128, hp * 128:(hp + 1) * 128])
                     for dc in range(DC)]
                    + [(wks[par][:, dc * 128:(dc + 1) * 128],
                        d_wqkv[dc * 128:(dc + 1) * 128, D + hp * 128:D + (hp + 1) * 128])
                       for dc in range(DC)],
                    f"gq{par}", waits=[("pe", p.c("pe"))])
                for tb in range(NTB):
                    t0, t1 = tb * TB, (tb + 1) * TB

                    def _(t, par=par, t0=t0, t1=t1):
                        for dc in range(DC):
                            t.matmul(pp["mm0"][:, :], wqs[par][:, dc * 128:(dc + 1) * 128],
                                     xn[dc][:, t0:t1], start=(dc == 0), stop=(dc == DC - 1))
                        last = None
                        for dc in range(DC):
                            last = t.matmul(pp["mm1"][:, :], wks[par][:, dc * 128:(dc + 1) * 128],
                                            xn[dc][:, t0:t1], start=(dc == 0),
                                            stop=(dc == DC - 1))
                        return last
                    pe_qk = p.emit("tensor", _,
                                   waits=[(f"gq{par}", gq), ("dv", p.c("dv")),
                                          ("ac", p.c("ac"))], inc="pe")

                    def _(v, hp=hp, t0=t0, t1=t1):
                        qc0 = bq_cols[0:64, hp:hp + 1]
                        qc1 = bq_cols[64:128, hp:hp + 1]
                        kc0 = bq_cols[0:64, 16 + hp:17 + hp]
                        kc1 = bq_cols[64:128, 16 + hp:17 + hp]
                        v.tensor_scalar(QA[:, t0:t1], pp["mm0"][0:64, :], qc0, None, ALU.add)
                        v.tensor_scalar(QB[:, t0:t1], pp["mm0"][64:128, :], qc1, None, ALU.add)
                        v.tensor_scalar(MA[:, t0:t1], pp["mm0"][0:64, :], qc0, -2.0,
                                        ALU.add, ALU.mult)
                        v.tensor_scalar(MB[:, t0:t1], pp["mm0"][64:128, :], qc1, -2.0,
                                        ALU.add, ALU.mult)
                        v.tensor_scalar(KA[:, t0:t1], pp["mm1"][0:64, :], kc0, None, ALU.add)
                        return v.tensor_scalar(KB[:, t0:t1], pp["mm1"][64:128, :], kc1,
                                               None, ALU.add)
                    p.emit("vector", _, waits=[("pe", pe_qk)])
                dv_qk = p.c("dv")

                for hh in range(2):
                    h = 2 * hp + hh
                    Qh = QA if hh == 0 else QB
                    Mh = MA if hh == 0 else MB
                    Kh = KA if hh == 0 else KB

                    def _(s_, Qh=Qh):
                        return s_.activation(Qsq[:, :], Qh[:, :], AF.Square)
                    ac_q = p.emit("scalar", _, waits=[("dv", dv_qk), ("pe", p.c("pe"))])
                    for tb in range(NTB):
                        def _(t, tb=tb):
                            return t.matmul(pp["rowA"][:, :], cst["ones_col_f"][0:64, :],
                                            Qsq[:, tb * TB:(tb + 1) * TB],
                                            start=True, stop=True)
                        pe_q = p.emit("tensor", _, waits=[("ac", ac_q), ("dv", p.c("dv"))],
                                      inc="pe")

                        def _(v, tb=tb):
                            return v.tensor_copy(qnb[0:1, tb * TB:(tb + 1) * TB],
                                                 pp["rowA"][:, :])
                        p.emit("vector", _, waits=[("pe", pe_q)])

                    def _(s_, Kh=Kh):
                        return s_.activation(Qsq[:, :], Kh[:, :], AF.Square)
                    ac_k = p.emit("scalar", _, waits=[("pe", p.c("pe")), ("dv", p.c("dv"))])
                    for tb in range(NTB):
                        def _(t, tb=tb):
                            return t.matmul(pp["rowB"][:, :], cst["ones_col_f"][0:64, :],
                                            Qsq[:, tb * TB:(tb + 1) * TB],
                                            start=True, stop=True)
                        pe_k = p.emit("tensor", _, waits=[("ac", ac_k), ("dv", p.c("dv"))],
                                      inc="pe")

                        def _(v, tb=tb):
                            return v.tensor_scalar(knb[0:1, tb * TB:(tb + 1) * TB],
                                                   pp["rowB"][:, :], 1e-6, None, ALU.add)
                        p.emit("vector", _, waits=[("pe", pe_k)])
                    dv_rows = p.c("dv")

                    for tq in range(T // 128):
                        q0 = tq * 128
                        ext = (tq + 1) * 128
                        nblk = (ext + TB - 1) // TB
                        dv_blk0 = p.c("dv")
                        ac_blk0 = p.c("ac")
                        for blk in range(nblk):
                            b0 = blk * TB
                            bw = min(TB, ext - b0)
                            psd = pp["mm0"] if blk % 2 == 0 else pp["mm1"]
                            psD = pp["bc0"] if blk % 2 == 0 else pp["bc1"]

                            def _(t, Qh=Qh, Mh=Mh, Kh=Kh, q0=q0, b0=b0, bw=bw,
                                  psd=psd, psD=psD, tq=tq):
                                t.matmul(psd[:, 0:bw], Qh[:, q0:q0 + 128],
                                         Kh[:, b0:b0 + bw], start=True, stop=True)
                                t.matmul(psD[:, 0:bw], qnb[0:1, q0:q0 + 128],
                                         cst["ones_tb_b"][0:1, 0:bw], start=True, stop=False)
                                t.matmul(psD[:, 0:bw], cst["ones_row_b"][:, :],
                                         knb[0:1, b0:b0 + bw], start=False, stop=False)
                                return t.matmul(psD[:, 0:bw], Mh[:, q0:q0 + 128],
                                                Kh[:, b0:b0 + bw], start=False, stop=True)
                            pe_s = p.emit("tensor", _,
                                          waits=[("dv", dv_blk0 if blk < 2 else p.c("dv")),
                                                 ("ac", ac_blk0 if blk < 2 else p.c("ac")),
                                                 ("dv", dv_rows)], inc="pe")

                            def _(s_, b0=b0, bw=bw, psd=psd):
                                return s_.activation(dot2[:, 0:bw], psd[:, 0:bw], AF.Square)
                            p.emit("scalar", _, waits=[("pe", pe_s), ("dv", p.c("dv"))])

                            def _(v, b0=b0, bw=bw, psD=psD):
                                return v.reciprocal(rcp[:, 0:bw], psD[:, 0:bw])
                            p.emit("vector", _, waits=[("pe", pe_s)])

                            def _(v, b0=b0, bw=bw):
                                return v.tensor_mul(score_sb[:, b0:b0 + bw],
                                                    dot2[:, 0:bw], rcp[:, 0:bw])
                            p.emit("vector", _, waits=[("ac", p.c("ac"))])
                        dv_sc = p.c("dv")

                        def _(g, q0=q0):
                            return g.affine_select(score_sb[:, q0:q0 + 128],
                                                   score_sb[:, q0:q0 + 128],
                                                   [[-1, 128]], ALU.is_ge,
                                                   maskreg.get("r", 0.0),
                                                   base=0, channel_multiplier=1)
                        po_m = p.emit("gpsimd", _, waits=[("dv", dv_sc)])

                        def _(v, ext=ext):
                            return v.reduce_max(mcol[:, :], score_sb[:, 0:ext], AX.X,
                                                negate=True)
                        p.emit("vector", _, waits=[("po", po_m)])

                        def _(s_, ext=ext):
                            return s_.activation(Pb[:, 0:ext], score_sb[:, 0:ext], AF.Exp,
                                                 bias=mcol[:, :], accum_out=lcol[:, :])
                        ac_e1 = p.emit("scalar", _, waits=[("dv", p.c("dv"))])

                        def _(s_):
                            return s_.activation(b2col[:, :], lcol[:, :], AF.Ln)
                        p.emit("scalar", _)

                        def _(v):
                            return v.tensor_sub(b2col[:, :], mcol[:, :], b2col[:, :])
                        dv_b2 = p.emit("vector", _, waits=[("ac", p.c("ac"))])

                        def _(s_, ext=ext):
                            return s_.activation(Pb[:, 0:ext], score_sb[:, 0:ext], AF.Exp,
                                                 bias=b2col[:, :])
                        ac_e2 = p.emit("scalar", _, waits=[("dv", dv_b2)])

                        dv_ptprev = p.c("dv")
                        for kc in range(tq + 1):
                            tb_ = tpb[kc % 2]

                            def _(t, kc=kc, tb_=tb_):
                                return t.transpose(tb_[:, :], Pb[:, kc * 128:(kc + 1) * 128],
                                                   cst["id_b"][:, :])
                            pe_p = p.emit("tensor", _,
                                          waits=[("ac", ac_e2), ("dv", p.c("dv"))], inc="pe")

                            def _(v, kc=kc, tb_=tb_):
                                return v.tensor_copy(PT[:, kc * 128:(kc + 1) * 128], tb_[:, :])
                            p.emit("vector", _, waits=[("pe", pe_p)])
                        dv_pt = p.c("dv")

                        def _(t, h=h, hh=hh, tq=tq):
                            last = None
                            for kc in range(tq + 1):
                                last = t.matmul(po_[64 * hh:64 * hh + 64, :],
                                                V_tm[:, kc * 1024 + 64 * h:
                                                     kc * 1024 + 64 * h + 64],
                                                PT[:, kc * 128:(kc + 1) * 128],
                                                start=(kc == 0), stop=(kc == tq))
                            return last
                        pe_av = p.emit("tensor", _, waits=[("dv", dv_pt)], inc="pe")

                        def _(v, hp=hp, hh=hh, tq=tq):
                            return v.tensor_copy(
                                O_fm[hp][64 * hh:64 * hh + 64, tq * 128:(tq + 1) * 128],
                                po_[64 * hh:64 * hh + 64, :])
                        p.emit("vector", _, waits=[("pe", pe_av)])

            for dcO in range(DC):
                for tb in range(NTB):
                    t0, t1 = tb * TB, (tb + 1) * TB
                    ps = pp["mm0"] if (dcO * NTB + tb) % 2 == 0 else pp["mm1"]
                    par = (dcO * NTB + tb) % 2
                    xw = dma_batch(p, "sync",
                                   [(xchunk[par][:, :], d_xfm[s, dcO * 128:(dcO + 1) * 128,
                                                              t0:t1])],
                                   "lxc", waits=[("dv", p.c("dv"))])

                    def _(t, dcO=dcO, t0=t0, t1=t1, ps=ps):
                        t.matmul(ps[:, :], bo_row[0:1, dcO * 128:(dcO + 1) * 128],
                                 cst["ones_tb_b"][:, :], start=True, stop=False)
                        last = None
                        for dc in range(DC):
                            last = t.matmul(ps[:, :], wo[dc][:, dcO * 128:(dcO + 1) * 128],
                                            O_fm[dc][:, t0:t1], start=False,
                                            stop=(dc == DC - 1))
                        return last
                    pe_o = p.emit("tensor", _, waits=[("dv", p.c("dv")), ("gsm", gsm),
                                                      ("ac", p.c("ac"))], inc="pe")

                    def _(v, par=par, ps=ps):
                        return v.tensor_add(obuf[par][:, :], xchunk[par][:, :], ps[:, :])
                    p.emit("vector", _, waits=[("pe", pe_o), ("lxc", xw),
                                               ("st", p.c("st"))])

                    dma_batch(p, "sync",
                              [(d_x1[s, dcO * 128:(dcO + 1) * 128, t0:t1], obuf[par][:, :])],
                              "st", waits=[("dv", p.c("dv"))])

        p.emit("sync", lambda y: None, waits=[("st", p.c("st"))])
        p.build()
    return nc


TC = 256
NCH = T // TC
NBLK = 4
EC = 16
NST = 16


def emit_ln_stats(p, cst, pp, lnp, x_tiles, wait_x):
    sq, srow, qrow = lnp["sq"], lnp["srow"], lnp["qrow"]
    mrow, irow = lnp["srow"], lnp["irow"]
    psA, psB = pp["rowA"], pp["rowB"]
    for tb in range(NTB):
        t0, t1 = tb * TB, (tb + 1) * TB
        dv_now = p.c("dv")

        def _(t, t0=t0, t1=t1):
            last = None
            for dc in range(DC):
                last = t.matmul(psA[0:1, :], cst["ones_col_f"][:, :],
                                x_tiles[dc][:, t0:t1], start=(dc == 0), stop=(dc == DC - 1))
            return last
        pe_v = p.emit("tensor", _, waits=list(wait_x) + cst["ready"] + [("dv", dv_now)],
                      inc="pe")

        def _(v, t0=t0, t1=t1):
            return v.tensor_copy(srow[:, t0:t1], psA[0:1, :])
        p.emit("vector", _, waits=[("pe", pe_v)])

        for dc in range(DC):
            def _(s, dc=dc, t0=t0, t1=t1):
                return s.activation(sq[:, :], x_tiles[dc][:, t0:t1], AF.Square)
            ac_v = p.emit("scalar", _, waits=list(wait_x) + [("pe", p.c("pe"))])

            def _(t, dc=dc):
                return t.matmul(psB[0:1, :], cst["ones_col_f"][:, :], sq[:, :],
                                start=(dc == 0), stop=(dc == DC - 1))
            p.emit("tensor", _, waits=[("ac", ac_v), ("dv", p.c("dv"))], inc="pe")

        def _(v, t0=t0, t1=t1):
            return v.tensor_copy(qrow[:, t0:t1], psB[0:1, :])
        p.emit("vector", _, waits=[("pe", p.c("pe"))])

    def _(v):
        v.tensor_scalar_mul(mrow[:, :], srow[:, :], 1.0 / D)
        v.tensor_scalar_mul(qrow[:, :], qrow[:, :], 1.0 / D)
        v.tensor_mul(irow[:, :], mrow[:, :], mrow[:, :])
        return v.tensor_sub(qrow[:, :], qrow[:, :], irow[:, :])
    dv_var = p.emit("vector", _, waits=[("dv", p.c("dv"))])

    def _(s):
        return s.activation(qrow[:, :], qrow[:, :], AF.Sqrt, bias=cst["eps_row"][:, :])
    ac_std = p.emit("scalar", _, waits=[("dv", dv_var)])

    def _(v):
        v.reciprocal(irow[:, :], qrow[:, :])
        return v.tensor_mul(mrow[:, :], mrow[:, :], irow[:, :])
    dv_i = p.emit("vector", _, waits=[("ac", ac_std)])
    return [("dv", dv_i)]


def emit_ln_apply(p, cst, pp, lnp, x_tiles, out_tiles, t0, t1, wait_rows,
                  gamma_col=None, beta_col=None, sq_view=None, out_full=True):
    w = t1 - t0
    irow, mrow = lnp["irow"], lnp["srow"]
    sq = lnp["sq"] if sq_view is None else sq_view
    dv_prev = p.c("dv")

    def _(t, t0=t0, t1=t1, w=w):
        t.matmul(pp["bc0"][:, 0:w], cst["ones_row_f"][:, :], irow[0:1, t0:t1],
                 start=True, stop=True)
        return t.matmul(pp["bc1"][:, 0:w], cst["ones_row_f"][:, :], mrow[0:1, t0:t1],
                        start=True, stop=True)
    pe_bc = p.emit("tensor", _, waits=[("dv", dv_prev)] + list(wait_rows), inc="pe")

    for dc in range(DC):
        if gamma_col is None:
            def _(v, dc=dc, t0=t0, t1=t1, w=w):
                v.tensor_mul(sq[:, 0:w], x_tiles[dc][:, t0:t1], pp["bc0"][:, 0:w])
                return v.tensor_sub(out_tiles[dc][:, 0:t1 - t0] if out_tiles[dc].shape[-1] == t1 - t0
                                    else out_tiles[dc][:, t0:t1], sq[:, 0:w], pp["bc1"][:, 0:w])
        else:
            def _(v, dc=dc, t0=t0, t1=t1, w=w):
                v.tensor_mul(sq[:, 0:w], x_tiles[dc][:, t0:t1], pp["bc0"][:, 0:w])
                v.tensor_sub(sq[:, 0:w], sq[:, 0:w], pp["bc1"][:, 0:w])
                od = out_tiles[dc][:, 0:t1 - t0] if out_tiles[dc].shape[-1] == t1 - t0 \
                    else out_tiles[dc][:, t0:t1]
                return v.tensor_scalar(od, sq[:, 0:w], gamma_col[dc], beta_col[dc],
                                       ALU.mult, ALU.add)
        p.emit("vector", _, waits=[("pe", pe_bc)])
    return [("dv", p.c("dv"))]


def build_l1attn_launch():
    nc = bass.Bass("TRN2", target_bir_lowering=False)
    DI = 2048
    d_x2 = nc.dram_tensor("x2", [NSEQ, D, T], F32, kind="ExternalInput")
    d_ipw = nc.dram_tensor("ipw", [NBLK, 32, D, 128], F32, kind="ExternalInput")
    d_opw = nc.dram_tensor("opw", [NBLK, 8, DI, 128], F32, kind="ExternalInput")
    d_xpw = nc.dram_tensor("xpw", [NBLK, DI, 96], F32, kind="ExternalInput")
    d_dtw = nc.dram_tensor("dtw", [NBLK, 64, DI], F32, kind="ExternalInput")
    d_cw = nc.dram_tensor("cw", [NBLK, DI, 4], F32, kind="ExternalInput")
    d_cb = nc.dram_tensor("cb", [NBLK, EC, 128], F32, kind="ExternalInput")
    d_dtb = nc.dram_tensor("dtb", [NBLK, EC, 128], F32, kind="ExternalInput")
    d_alog = nc.dram_tensor("alog", [NBLK, DI, NST], F32, kind="ExternalInput")
    d_skd = nc.dram_tensor("skd", [NBLK, EC, 128], F32, kind="ExternalInput")
    d_nw = nc.dram_tensor("nw", [NBLK, DC, 128], F32, kind="ExternalInput")
    d_nb = nc.dram_tensor("nb", [NBLK, DC, 128], F32, kind="ExternalInput")
    d_fnw = nc.dram_tensor("fnw", [DC, 128], F32, kind="ExternalInput")
    d_fnb = nc.dram_tensor("fnb", [DC, 128], F32, kind="ExternalInput")
    d_x3 = nc.dram_tensor("x3", [NSEQ, D, T], F32, kind="ExternalOutput")

    p = Prog(nc)
    with p.stack:
        cst = common_consts(p)
        pp = {
            "rowA": p.psum("pp_rowA", (1, TB)), "rowB": p.psum("pp_rowB", (1, TB)),
            "bc0": p.psum("pp_bc0", (128, TB)), "bc1": p.psum("pp_bc1", (128, TB)),
            "mm0": p.psum("pp_mm0", (128, TB)), "mm1": p.psum("pp_mm1", (128, TB)),
        }
        px1 = p.psum("pp_px1", (80, TC))
        px2 = p.psum("pp_px2", (16, TC))

        xm = [p.sbuf(f"xm_{dc}", (128, T)) for dc in range(DC)]
        USZ = 27648
        u = p.sbuf("uscr", (128, USZ))
        uh = u
        lnp = {"sq": u[:, 0:512], "srow": u[0:1, 512:2560],
               "qrow": u[0:1, 2560:4608], "irow": u[0:1, 4608:6656]}
        CB = 6656
        o_us = CB
        u_s = [u[:, o_us + ec * TC:o_us + (ec + 1) * TC] for ec in range(EC)]
        o_usb = CB + 4096
        usb = [u[:, o_usb + ec * 128:o_usb + (ec + 1) * 128].bitcast(BF16)
               for ec in range(EC)]
        o_zs = CB + 6144
        zs = [u[:, o_zs + ec * 128:o_zs + (ec + 1) * 128].bitcast(BF16)
              for ec in range(EC)]
        o_bbc = CB + 8192
        Bbc = u[:, o_bbc:o_bbc + 2048].bitcast(BF16)
        o_cbc = CB + 10240
        Cbc = u[:, o_cbc:o_cbc + 2048].bitcast(BF16)
        o_ybf = CB + 12288
        y_bf = [u[:, o_ybf + ec * 128:o_ybf + (ec + 1) * 128].bitcast(BF16)
                for ec in range(EC)]
        o_da = CB + 14336
        dA4 = u[:, o_da:o_da + 4 * TC]
        o_dbu = CB + 15360
        dBu4 = u[:, o_dbu:o_dbu + 4 * TC]
        o_h4 = CB + 16384
        h4 = u[:, o_h4:o_h4 + 4 * TC]
        o_dt = CB + 17408
        dt_ec = u[:, o_dt:o_dt + TC]
        o_dtu = CB + 17664
        dtu_ec = u[:, o_dtu:o_dtu + TC]
        o_acc = CB + 17920
        acc_c = u[:, o_acc:o_acc + TC]
        o_ext = CB + 18176
        ext_c = u[:, o_ext:o_ext + TC + 4]
        o_ye = CB + 18436
        y_ec = u[:, o_ye:o_ye + TC]
        o_yt = CB + 18692
        ytmp = u[:, o_yt:o_yt + TC]
        o_sp0 = CB + 18948
        sp0 = u[:, o_sp0:o_sp0 + TC]
        o_sp1 = CB + 19204
        sp1 = u[:, o_sp1:o_sp1 + TC]
        o_dtr = CB + 19460
        dtraw = u[:, o_dtr:o_dtr + 128].bitcast(BF16)[0:64, 0:TC]
        o_bsb = CB + 19588
        B_sb = u[0:16, o_bsb:o_bsb + 128].bitcast(BF16)[0:16, 0:TC]
        o_csb = CB + 19716
        C_sb = u[0:16, o_csb:o_csb + 128].bitcast(BF16)[0:16, 0:TC]
        o_xnc = CB + 19844
        xn_c = [u[:, o_xnc + dc * 128:o_xnc + (dc + 1) * 128].bitcast(BF16)
                for dc in range(DC)]
        chbuf = [u[:, o_us + dc * TB:o_us + (dc + 1) * TB] for dc in range(DC)]
        xchunk = [u[:, o_us + b * TB:o_us + (b + 1) * TB] for b in range(2)]
        obuf = [u[:, o_us + (2 + b) * TB:o_us + (3 + b) * TB] for b in range(2)]

        ipwb = [p.sbuf(f"ipwb{b}", (128, 1024), BF16) for b in range(2)]
        opwb = [p.sbuf(f"opwb{b}", (128, 2048), BF16) for b in range(2)]
        xpwb = p.sbuf("xpwb", (128, EC * 96), BF16)
        dtwb = p.sbuf("dtwb", (64, DI), BF16)
        cwt = p.sbuf("cwt", (128, EC * 4))
        cbt = p.sbuf("cbt", (128, EC))
        dtbt = p.sbuf("dtbt", (128, EC))
        skdt = p.sbuf("skdt", (128, EC))
        nwt = p.sbuf("nwt", (128, NBLK * DC))
        nbt = p.sbuf("nbt", (128, NBLK * DC))
        fnwt = p.sbuf("fnwt", (128, DC))
        fnbt = p.sbuf("fnbt", (128, DC))
        negA = p.sbuf("negA", (128, EC * NST))
        selT = p.sbuf("selT", (16, DI), BF16)
        st_all = p.sbuf("st_all", (128, NBLK * EC * NST))
        tails = p.sbuf("tails", (128, NBLK * EC * 4))

        def _(g):
            g.memset(selT[:, :], 1.0)
            g.affine_select(selT[:, :], selT[:, :], [[1, DI]], ALU.is_ge, 0.0,
                            base=0, channel_multiplier=-128)
            return g.affine_select(selT[:, :], selT[:, :], [[-1, DI]], ALU.is_ge, 0.0,
                                   base=127, channel_multiplier=128)
        p.emit("gpsimd", _)

        gsm = dma_batch(p, "gpsimd",
                        [(nwt[:, b * DC + dc:b * DC + dc + 1], d_nw[b, dc:dc + 1, :]
                          .rearrange("o i -> i o")) for b in range(NBLK) for dc in range(DC)]
                        + [(nbt[:, b * DC + dc:b * DC + dc + 1], d_nb[b, dc:dc + 1, :]
                            .rearrange("o i -> i o")) for b in range(NBLK) for dc in range(DC)]
                        + [(fnwt[:, dc:dc + 1], d_fnw[dc:dc + 1, :].rearrange("o i -> i o"))
                           for dc in range(DC)]
                        + [(fnbt[:, dc:dc + 1], d_fnb[dc:dc + 1, :].rearrange("o i -> i o"))
                           for dc in range(DC)],
                        "gsm")

        for s in range(NSEQ):
            def _(v):
                v.memset(st_all[:, :], 0.0)
                return v.memset(tails[:, :], 0.0)
            p.emit("vector", _, waits=[("pe", p.c("pe"))])

            for tb in range(NTB):
                t0, t1 = tb * TB, (tb + 1) * TB
                lv = dma_batch(p, "sync",
                               [(chbuf[dc][:, :], d_x2[s, dc * 128:(dc + 1) * 128, t0:t1])
                                for dc in range(DC)],
                               "ld", waits=[("pe", p.c("pe")), ("dv", p.c("dv")),
                                            ("ac", p.c("ac")), ("st", p.c("st"))])

                def _(t, tb=tb):
                    last = None
                    for dc in range(DC):
                        last = t.matmul(pp["rowA"][0:1, :], cst["ones_col_f"][:, :],
                                        chbuf[dc][:, :], start=(dc == 0), stop=(dc == DC - 1))
                    return last
                pe_v = p.emit("tensor", _, waits=[("ld", lv), ("dv", p.c("dv"))]
                              + cst["ready"], inc="pe")

                def _(v, t0=t0, t1=t1):
                    return v.tensor_copy(lnp["srow"][0:1, t0:t1], pp["rowA"][0:1, :])
                p.emit("vector", _, waits=[("pe", pe_v)])
                for dc in range(DC):
                    def _(s_, dc=dc):
                        return s_.activation(lnp["sq"][:, :], chbuf[dc][:, :], AF.Square)
                    ac_v = p.emit("scalar", _, waits=[("pe", p.c("pe"))])

                    def _(t, dc=dc):
                        return t.matmul(pp["rowB"][0:1, :], cst["ones_col_f"][:, :],
                                        lnp["sq"][:, :], start=(dc == 0), stop=(dc == DC - 1))
                    p.emit("tensor", _, waits=[("ac", ac_v), ("dv", p.c("dv"))], inc="pe")

                def _(v, t0=t0, t1=t1):
                    return v.tensor_copy(lnp["qrow"][0:1, t0:t1], pp["rowB"][0:1, :])
                p.emit("vector", _, waits=[("pe", p.c("pe"))])

            def _(v):
                v.tensor_scalar_mul(lnp["srow"][:, :], lnp["srow"][:, :], 1.0 / D)
                v.tensor_scalar_mul(lnp["qrow"][:, :], lnp["qrow"][:, :], 1.0 / D)
                v.tensor_mul(lnp["irow"][:, :], lnp["srow"][:, :], lnp["srow"][:, :])
                return v.tensor_sub(lnp["qrow"][:, :], lnp["qrow"][:, :], lnp["irow"][:, :])
            dv_var = p.emit("vector", _, waits=[("dv", p.c("dv"))])

            def _(s_):
                return s_.activation(lnp["qrow"][:, :], lnp["qrow"][:, :], AF.Sqrt,
                                     bias=cst["eps_row"][:, :])
            ac_std = p.emit("scalar", _, waits=[("dv", dv_var)])

            def _(v):
                v.reciprocal(lnp["irow"][:, :], lnp["qrow"][:, :])
                return v.tensor_mul(lnp["srow"][:, :], lnp["srow"][:, :], lnp["irow"][:, :])
            dv_rows0 = p.emit("vector", _, waits=[("ac", ac_std)])

            for tb in range(NTB):
                t0, t1 = tb * TB, (tb + 1) * TB
                lv = dma_batch(p, "sync",
                               [(chbuf[dc][:, :], d_x2[s, dc * 128:(dc + 1) * 128, t0:t1])
                                for dc in range(DC)],
                               "ld", waits=[("pe", p.c("pe")), ("dv", p.c("dv")),
                                            ("ac", p.c("ac")), ("st", p.c("st"))])

                def _(t, t0=t0, t1=t1):
                    t.matmul(pp["bc0"][:, :], cst["ones_row_f"][:, :], lnp["irow"][0:1, t0:t1],
                             start=True, stop=True)
                    return t.matmul(pp["bc1"][:, :], cst["ones_row_f"][:, :],
                                    lnp["srow"][0:1, t0:t1], start=True, stop=True)
                pe_bc = p.emit("tensor", _, waits=[("ld", lv), ("dv", p.c("dv")),
                                                   ("dv", dv_rows0)], inc="pe")
                for dc in range(DC):
                    def _(v, dc=dc, t0=t0, t1=t1):
                        v.tensor_mul(chbuf[dc][:, :], chbuf[dc][:, :], pp["bc0"][:, :])
                        return v.tensor_sub(xm[dc][:, t0:t1], chbuf[dc][:, :], pp["bc1"][:, :])
                    p.emit("vector", _, waits=[("pe", pe_bc)])

            for b in range(NBLK):
                gb = dma_batch(p, "gpsimd",
                               [(xpwb[:, :], d_xpw[b].rearrange("(eh p) j -> p eh j", p=128)),
                                (dtwb[:, :], d_dtw[b]),
                                (cwt[:, :], d_cw[b].rearrange("(eh p) k -> p eh k", p=128)),
                                (negA[:, :], d_alog[b].rearrange("(eh p) n -> p eh n", p=128))]
                               + [(cbt[:, ec:ec + 1], d_cb[b, ec:ec + 1, :]
                                   .rearrange("o i -> i o")) for ec in range(EC)]
                               + [(dtbt[:, ec:ec + 1], d_dtb[b, ec:ec + 1, :]
                                   .rearrange("o i -> i o")) for ec in range(EC)]
                               + [(skdt[:, ec:ec + 1], d_skd[b, ec:ec + 1, :]
                                   .rearrange("o i -> i o")) for ec in range(EC)],
                               "gcb", waits=[("ac", p.c("ac")), ("dv", p.c("dv")),
                                             ("pe", p.c("pe"))])

                def _(s_):
                    return s_.activation(negA[:, :], negA[:, :], AF.Exp)
                ac_a = p.emit("scalar", _, waits=[("gcb", gb)])

                def _(v):
                    return v.tensor_scalar_mul(negA[:, :], negA[:, :], -1.0)
                dv_a = p.emit("vector", _, waits=[("ac", ac_a)])

                gcol = [nwt[:, b * DC + dc:b * DC + dc + 1] for dc in range(DC)]
                bcol = [nbt[:, b * DC + dc:b * DC + dc + 1] for dc in range(DC)]
                wait_rows = emit_ln_stats(p, cst, pp, lnp, xm,
                                          [("gsm", gsm), ("dv", p.c("dv"))])

                for c in range(NCH):
                    c0, c1 = c * TC, (c + 1) * TC
                    wait_xn = emit_ln_apply(p, cst, pp, lnp, xm, xn_c, c0, c1,
                                            wait_rows, gamma_col=gcol, beta_col=bcol,
                                            sq_view=acc_c, out_full=False)

                    for ec in range(EC):
                        par = ec % 2
                        gi = dma_batch(p, "gpsimd",
                                       [(ipwb[par][:, :],
                                         d_ipw[b, ec].rearrange("(dh p) c -> p dh c", p=128))],
                                       f"gi{par}", waits=[("pe", p.c("pe"))])
                        ps = pp["mm0"] if par == 0 else pp["mm1"]

                        def _(t, ec=ec, par=par, ps=ps):
                            last = None
                            for dc in range(DC):
                                last = t.matmul(ps[:, 0:TC], ipwb[par][:, dc * 128:(dc + 1) * 128],
                                                xn_c[dc][:, :], start=(dc == 0),
                                                stop=(dc == DC - 1))
                            return last
                        pe_u = p.emit("tensor", _,
                                      waits=[(f"gi{par}", gi), ("dv", p.c("dv")),
                                             ("ac", p.c("ac"))] + list(wait_xn), inc="pe")

                        def _(v, b=b, ec=ec, ps=ps):
                            v.tensor_copy(ext_c[:, 0:4][:, 0:3],
                                          tails[:, (b * EC + ec) * 4:(b * EC + ec) * 4 + 3])
                            v.tensor_copy(ext_c[:, 3:3 + TC], ps[:, 0:TC])
                            v.tensor_copy(tails[:, (b * EC + ec) * 4:(b * EC + ec) * 4 + 3],
                                          ext_c[:, TC:TC + 3])
                            v.tensor_scalar(acc_c[:, :], ext_c[:, 3:3 + TC],
                                            cwt[:, ec * 4 + 3:ec * 4 + 4],
                                            cbt[:, ec:ec + 1], ALU.mult, ALU.add)
                            v.scalar_tensor_tensor(acc_c[:, :], ext_c[:, 0:TC],
                                                   cwt[:, ec * 4:ec * 4 + 1], acc_c[:, :],
                                                   ALU.mult, ALU.add)
                            v.scalar_tensor_tensor(acc_c[:, :], ext_c[:, 1:1 + TC],
                                                   cwt[:, ec * 4 + 1:ec * 4 + 2], acc_c[:, :],
                                                   ALU.mult, ALU.add)
                            return v.scalar_tensor_tensor(acc_c[:, :], ext_c[:, 2:2 + TC],
                                                          cwt[:, ec * 4 + 2:ec * 4 + 3],
                                                          acc_c[:, :], ALU.mult, ALU.add)
                        dv_cv = p.emit("vector", _, waits=[("pe", pe_u), ("gcb", gb),
                                                           ("ac", p.c("ac"))])

                        def _(s_, ec=ec):
                            return s_.activation(u_s[ec][:, :], acc_c[:, :], AF.Silu)
                        ac_si = p.emit("scalar", _, waits=[("dv", dv_cv)])

                        def _(v, ec=ec):
                            return v.tensor_copy(usb[ec][:, :], u_s[ec][:, :])
                        p.emit("vector", _, waits=[("ac", ac_si)])

                    for ec in range(EC):
                        par = ec % 2
                        gi = dma_batch(p, "gpsimd",
                                       [(ipwb[par][:, :],
                                         d_ipw[b, 16 + ec].rearrange("(dh p) c -> p dh c",
                                                                     p=128))],
                                       f"gi{par}", waits=[("pe", p.c("pe"))])
                        ps = pp["mm0"] if par == 0 else pp["mm1"]

                        def _(t, ec=ec, par=par, ps=ps):
                            last = None
                            for dc in range(DC):
                                last = t.matmul(ps[:, 0:TC], ipwb[par][:, dc * 128:(dc + 1) * 128],
                                                xn_c[dc][:, :], start=(dc == 0),
                                                stop=(dc == DC - 1))
                            return last
                        pe_z = p.emit("tensor", _,
                                      waits=[(f"gi{par}", gi), ("dv", p.c("dv")),
                                             ("ac", p.c("ac"))] + list(wait_xn), inc="pe")

                        def _(s_, ec=ec, ps=ps):
                            return s_.activation(zs[ec][:, :], ps[:, 0:TC], AF.Silu)
                        p.emit("scalar", _, waits=[("pe", pe_z)])

                    def _(t):
                        last = None
                        for ec in range(EC):
                            last = t.matmul(px1[:, :], xpwb[:, ec * 96:ec * 96 + 80],
                                            usb[ec][:, :], start=(ec == 0), stop=(ec == EC - 1))
                        for ec in range(EC):
                            last = t.matmul(px2[:, :], xpwb[:, ec * 96 + 80:(ec + 1) * 96],
                                            usb[ec][:, :], start=(ec == 0), stop=(ec == EC - 1))
                        return last
                    pe_xp = p.emit("tensor", _, waits=[("dv", p.c("dv")), ("gcb", gb),
                                                       ("ac", p.c("ac"))], inc="pe")

                    def _(v):
                        v.tensor_copy(dtraw[:, :], px1[0:64, :])
                        v.tensor_copy(B_sb[:, :], px1[64:80, :])
                        return v.tensor_copy(C_sb[:, :], px2[:, :])
                    dv_xp = p.emit("vector", _, waits=[("pe", pe_xp)])

                    for n in range(NST):
                        psn = pp["bc0"] if n % 2 == 0 else pp["bc1"]

                        def _(t, n=n, psn=psn):
                            t.matmul(psn[:, 0:TC], selT[:, n * 128:(n + 1) * 128],
                                     B_sb[:, :], start=True, stop=True)
                            return t.matmul(psn[:, TC:2 * TC], selT[:, n * 128:(n + 1) * 128],
                                            C_sb[:, :], start=True, stop=True)
                        pe_b = p.emit("tensor", _, waits=[("dv", p.c("dv")), ("dv", dv_xp)],
                                      inc="pe")

                        def _(v, n=n, psn=psn):
                            v.tensor_copy(Bbc[:, n * TC:(n + 1) * TC], psn[:, 0:TC])
                            return v.tensor_copy(Cbc[:, n * TC:(n + 1) * TC], psn[:, TC:2 * TC])
                        p.emit("vector", _, waits=[("pe", pe_b)])

                    for ec in range(EC):
                        def _(t, ec=ec):
                            return t.matmul(pp["mm0"][:, 0:TC],
                                            dtwb[:, ec * 128:(ec + 1) * 128],
                                            dtraw[:, :], start=True, stop=True)
                        pe_dt = p.emit("tensor", _, waits=[("dv", p.c("dv")),
                                                           ("ac", p.c("ac"))], inc="pe")

                        def _(v, ec=ec):
                            return v.tensor_scalar(sp0[:, :], pp["mm0"][:, 0:TC],
                                                   dtbt[:, ec:ec + 1], None, ALU.add)
                        dv_s0 = p.emit("vector", _, waits=[("pe", pe_dt)])

                        def _(s_):
                            return s_.activation(sp1[:, :], sp0[:, :], AF.Abs)
                        ac_s1 = p.emit("scalar", _, waits=[("dv", dv_s0)])

                        def _(s_):
                            return s_.activation(sp1[:, :], sp1[:, :], AF.Exp, scale=-1.0)
                        p.emit("scalar", _)

                        def _(s_):
                            return s_.activation(sp1[:, :], sp1[:, :], AF.Ln,
                                                 bias=cst["ones_col_f"][:, :])
                        ac_s3 = p.emit("scalar", _)

                        def _(v):
                            v.tensor_relu(dt_ec[:, :], sp0[:, :])
                            return v.tensor_add(dt_ec[:, :], dt_ec[:, :], sp1[:, :])
                        dv_dt = p.emit("vector", _, waits=[("ac", ac_s3)])

                        def _(v, ec=ec):
                            return v.tensor_mul(dtu_ec[:, :], dt_ec[:, :], u_s[ec][:, :])
                        dv_dtu = p.emit("vector", _)

                        for g in range(4):
                            def _(s_, b=b, ec=ec, g=g):
                                last = None
                                for j in range(4):
                                    n = 4 * g + j
                                    last = s_.activation(dA4[:, j * TC:(j + 1) * TC],
                                                         dt_ec[:, :], AF.Exp,
                                                         scale=negA[:, ec * NST + n:
                                                                    ec * NST + n + 1])
                                return last
                            ac_da = p.emit("scalar", _, waits=[("dv", dv_dt),
                                                               ("dv", p.c("dv"))])

                            def _(v, g=g):
                                rep = bass.AP(uh, o_dtu, [[USZ, 128], [0, 4], [1, TC]])
                                return v.tensor_tensor(dBu4[:, :], rep,
                                                       Bbc[:, 4 * g * TC:4 * (g + 1) * TC],
                                                       ALU.mult)
                            dv_db = p.emit("vector", _, waits=[("dv", dv_dtu)])

                            def _(v, b=b, ec=ec, g=g):
                                last = None
                                for j in range(4):
                                    n = 4 * g + j
                                    stc = b * EC * NST + ec * NST + n
                                    last = v.tensor_tensor_scan(
                                        h4[:, j * TC:(j + 1) * TC],
                                        dA4[:, j * TC:(j + 1) * TC],
                                        dBu4[:, j * TC:(j + 1) * TC],
                                        st_all[:, stc:stc + 1], ALU.mult, ALU.add)
                                return last
                            p.emit("vector", _, waits=[("ac", ac_da), ("dv", dv_db)])

                            def _(v, b=b, ec=ec, g=g):
                                stc = b * EC * NST + ec * NST + 4 * g
                                hcols = bass.AP(uh, o_h4 + TC - 1, [[USZ, 128], [TC, 4]])
                                return v.tensor_copy(st_all[:, stc:stc + 4], hcols)
                            p.emit("vector", _)

                            def _(v, g=g):
                                return v.tensor_mul(dBu4[:, :], h4[:, :],
                                                    Cbc[:, 4 * g * TC:4 * (g + 1) * TC])
                            p.emit("vector", _)

                            def _(v, g=g):
                                p3 = bass.AP(uh, o_dbu, [[USZ, 128], [1, TC], [TC, 4]])
                                return v.tensor_reduce(ytmp[:, :] if g > 0 else y_ec[:, :],
                                                       p3, AX.X, ALU.add)
                            p.emit("vector", _)
                            if g > 0:
                                def _(v):
                                    return v.tensor_add(y_ec[:, :], y_ec[:, :], ytmp[:, :])
                                p.emit("vector", _)

                        def _(v, ec=ec):
                            v.scalar_tensor_tensor(y_ec[:, :], u_s[ec][:, :],
                                                   skdt[:, ec:ec + 1], y_ec[:, :],
                                                   ALU.mult, ALU.add)
                            return v.tensor_mul(y_bf[ec][:, :], y_ec[:, :], zs[ec][:, :])
                        p.emit("vector", _, waits=[("ac", p.c("ac"))])
                    dv_y = p.c("dv")

                    for dcO in range(DC):
                        par = dcO % 2
                        go = dma_batch(p, "gpsimd",
                                       [(opwb[par][:, :],
                                         d_opw[b, dcO].rearrange("(eh p) c -> p eh c",
                                                                 p=128))],
                                       f"go{par}", waits=[("pe", p.c("pe"))])
                        ps = pp["mm0"] if par == 0 else pp["mm1"]

                        def _(t, dcO=dcO, par=par, ps=ps):
                            last = None
                            for ec in range(EC):
                                last = t.matmul(ps[:, 0:TC], opwb[par][:, ec * 128:(ec + 1) * 128],
                                                y_bf[ec][:, :], start=(ec == 0),
                                                stop=(ec == EC - 1))
                            return last
                        pe_o = p.emit("tensor", _, waits=[(f"go{par}", go), ("dv", dv_y),
                                                          ("dv", p.c("dv")),
                                                          ("ac", p.c("ac"))], inc="pe")

                        def _(v, dcO=dcO, c0=c0, c1=c1, ps=ps):
                            return v.tensor_add(xm[dcO][:, c0:c1], xm[dcO][:, c0:c1],
                                                ps[:, 0:TC])
                        p.emit("vector", _, waits=[("pe", pe_o)])

            fg = [fnwt[:, dc:dc + 1] for dc in range(DC)]
            fb = [fnbt[:, dc:dc + 1] for dc in range(DC)]
            wait_rows = emit_ln_stats(p, cst, pp, lnp, xm,
                                      [("gsm", gsm), ("dv", p.c("dv"))])
            for tb in range(NTB):
                emit_ln_apply(p, cst, pp, lnp, xm, xm, tb * TB, (tb + 1) * TB,
                              wait_rows, gamma_col=fg, beta_col=fb,
                              sq_view=None, out_full=True)

            for dc in range(DC):
                for tb in range(NTB):
                    t0, t1 = tb * TB, (tb + 1) * TB
                    par = (dc * NTB + tb) % 2
                    lx = dma_batch(p, "sync",
                                   [(xchunk[par][:, :], d_x2[s, dc * 128:(dc + 1) * 128,
                                                             t0:t1])],
                                   "lxc", waits=[("dv", p.c("dv")), ("st", p.c("st"))])

                    def _(v, dc=dc, t0=t0, t1=t1, par=par):
                        return v.tensor_add(obuf[par][:, :], xchunk[par][:, :],
                                            xm[dc][:, t0:t1])
                    p.emit("vector", _, waits=[("lxc", lx), ("st", p.c("st"))])

                    dma_batch(p, "sync",
                              [(d_x3[s, dc * 128:(dc + 1) * 128, t0:t1], obuf[par][:, :])],
                              "st", waits=[("dv", p.c("dv"))])

        p.emit("sync", lambda y: None, waits=[("st", p.c("st"))])
        p.build()
    return nc


def _shards(a, n=8):
    a = np.ascontiguousarray(a)
    return [a[NSEQ * i:NSEQ * (i + 1)] for i in range(n)]


_NC_CACHE = {}


def _get_nc(name, builder):
    if name not in _NC_CACHE:
        _NC_CACHE[name] = builder()
    return _NC_CACHE[name]


def kernel(**inputs):
    f32c = lambda a: np.ascontiguousarray(np.asarray(a, dtype=np.float32))
    x = f32c(inputs["L0_attn_x"])
    wqkv = f32c(inputs["L0_attn_w_qkv"])
    bq = f32c(inputs["L0_attn_b_qkv"]).reshape(1, -1)
    wo = f32c(inputs["L0_attn_w_out"])
    bo = f32c(inputs["L0_attn_b_out"]).reshape(1, -1)
    gwT = np.ascontiguousarray(f32c(inputs["L0_ffn_gate_w"]).T)
    w1 = f32c(inputs["L0_ffn_w1"]); w2 = f32c(inputs["L0_ffn_w2"])
    w3 = f32c(inputs["L0_ffn_w3"])
    ipw = np.ascontiguousarray(f32c(inputs["L1_attn_in_proj_w"])
                               .reshape(4, 1024, 32, 128).transpose(0, 2, 1, 3))
    opw = np.ascontiguousarray(f32c(inputs["L1_attn_out_proj_w"])
                               .reshape(4, 2048, 8, 128).transpose(0, 2, 1, 3))
    xpw = f32c(inputs["L1_attn_xproj_w"])
    dtw = f32c(inputs["L1_attn_dtproj_w"])
    cw = f32c(inputs["L1_attn_conv_w"])
    cb = np.ascontiguousarray(f32c(inputs["L1_attn_conv_b"]).reshape(4, 16, 128))
    dtb = np.ascontiguousarray(f32c(inputs["L1_attn_dtproj_b"]).reshape(4, 16, 128))
    alog = f32c(inputs["L1_attn_A_log"])
    skd = np.ascontiguousarray(f32c(inputs["L1_attn_skip_D"]).reshape(4, 16, 128))
    nw = np.ascontiguousarray(f32c(inputs["L1_attn_norm_w"]).reshape(4, 8, 128))
    nb = np.ascontiguousarray(f32c(inputs["L1_attn_norm_b"]).reshape(4, 8, 128))
    fnw = np.ascontiguousarray(f32c(inputs["L1_attn_fn_w"]).reshape(8, 128))
    fnb = np.ascontiguousarray(f32c(inputs["L1_attn_fn_b"]).reshape(8, 128))
    WeT = np.ascontiguousarray(f32c(inputs["L1_ffn_W_experts"]).transpose(0, 2, 1))
    be = f32c(inputs["L1_ffn_b_experts"])
    WrT = np.ascontiguousarray(f32c(inputs["L1_ffn_W_router"]).T)
    br = f32c(inputs["L1_ffn_b_router"]).reshape(1, 8)

    cores = list(range(8))
    xs = _shards(x)

    nc1 = _get_nc("l0attn", build_l0attn_launch)
    res = run_bass_kernel_spmd(nc1, [
        {"x": xs[i], "wqkv": wqkv, "bq": bq, "wo": wo, "bo": bo} for i in range(8)
    ], core_ids=cores)
    x1s = [np.asarray(res.results[i]["x1"]) for i in range(8)]

    nc2 = _get_nc("l0ffn", build_l0ffn_launch)
    res = run_bass_kernel_spmd(nc2, [
        {"x1": x1s[i], "gwT": gwT, "w1": w1, "w2": w2, "w3": w3} for i in range(8)
    ], core_ids=cores)
    x2s = [np.asarray(res.results[i]["x2"]) for i in range(8)]

    nc3 = _get_nc("l1attn", build_l1attn_launch)
    res = run_bass_kernel_spmd(nc3, [
        {"x2": x2s[i], "ipw": ipw, "opw": opw, "xpw": xpw, "dtw": dtw, "cw": cw,
         "cb": cb, "dtb": dtb, "alog": alog, "skd": skd, "nw": nw, "nb": nb,
         "fnw": fnw, "fnb": fnb} for i in range(8)
    ], core_ids=cores)
    x3s = [np.asarray(res.results[i]["x3"]) for i in range(8)]

    nc4 = _get_nc("l1ffn", build_l1ffn_launch)
    res = run_bass_kernel_spmd(nc4, [
        {"x3": x3s[i], "WeT": WeT, "be": be, "WrT": WrT, "br": br} for i in range(8)
    ], core_ids=cores)
    x4s = [np.asarray(res.results[i]["x4"]) for i in range(8)]

    return np.concatenate(x4s, axis=0)


# version 05
# speedup vs baseline: 1.0193x; 1.0193x over previous
import math
import numpy as np
from contextlib import ExitStack

import concourse.bass as bass
import concourse.mybir as mybir
from concourse.bass_utils import run_bass_kernel_spmd

F32 = mybir.dt.float32
BF16 = mybir.dt.bfloat16
AF = mybir.ActivationFunctionType
ALU = mybir.AluOpType
AX = mybir.AxisListType

D = 1024
T = 2048
NSEQ = 2
DC = D // 128
TB = 512
NTB = T // TB
LN_EPS = 1e-5


_CHAINED_OPS = frozenset({
    "memset", "memzero", "tensor_copy", "tensor_tensor", "tensor_add",
    "tensor_sub", "tensor_mul", "tensor_max", "tensor_scalar",
    "tensor_single_scalar", "tensor_scalar_add", "tensor_scalar_sub",
    "tensor_scalar_mul", "tensor_scalar_max", "tensor_scalar_min",
    "tensor_relu", "tensor_reduce", "reduce_max", "reduce_sum",
    "scalar_tensor_tensor", "tensor_tensor_scan", "select", "copy_predicated",
    "transpose", "bn_stats", "bn_aggr", "reciprocal", "stream_shuffle",
    "iota", "affine_select", "activation", "square", "sqrt", "mul", "add",
    "copy", "sign", "affine_then_add", "affine_mul_reduce",
    "tensor_tensor_reduce", "max", "max_index", "pool", "pool_avg", "pool_max",
})
_CHAIN_SEM = {"vector": "dv", "scalar": "ac", "gpsimd": "po"}


class _MockIns:
    def then_inc(self, *a, **k):
        return self

    def annotate(self, *a, **k):
        return self


class _CountMock:

    def __init__(self):
        self.n = 0

    def __getattr__(self, name):
        if name in _CHAINED_OPS:
            def f(*a, **k):
                self.n += 1
                return _MockIns()
        else:
            def f(*a, **k):
                return _MockIns()
        return f


class _ChainProxy:

    def __init__(self, eng, sem, state, strict=True):
        self._eng = eng
        self._sem = sem
        self._st = state
        self._strict = strict

    def __getattr__(self, name):
        attr = getattr(self._eng, name)
        if name in _CHAINED_OPS:
            if self._strict:
                def f(*a, **k):
                    if self._st["n"] > 0:
                        self._eng.wait_ge(self._sem, self._st["n"])
                    ins = attr(*a, **k)
                    ins.then_inc(self._sem, 1)
                    self._st["n"] += 1
                    return ins
            else:
                def f(*a, **k):
                    ins = attr(*a, **k)
                    self._st["pend"] += 1
                    self._st["last_ins"] = ins
                    return ins
            return f
        return attr

    def _commit(self):
        if not self._strict and self._st["pend"]:
            self._st["last_ins"].then_inc(self._sem, self._st["pend"])
            self._st["n"] += self._st["pend"]
            self._st["pend"] = 0
            self._st["last_ins"] = None


class Prog:

    ENGINES = ("sync", "vector", "scalar", "gpsimd", "tensor")

    def __init__(self, nc, strict_chains=True):
        self.nc = nc
        self.strict_chains = strict_chains
        self.stack = ExitStack()
        self.items = {e: [] for e in self.ENGINES}
        self.semh = {}
        self.semc = {}
        self._uid = 0

    def uid(self):
        self._uid += 1
        return self._uid

    def sem(self, name):
        if name not in self.semh:
            self.semh[name] = self.stack.enter_context(self.nc.semaphore(name))
            self.semc[name] = 0
        return self.semh[name]

    def sbuf(self, name, shape, dt=F32):
        return self.stack.enter_context(self.nc.sbuf_tensor(name, list(shape), dt))

    def psum(self, name, shape, dt=F32):
        return self.stack.enter_context(self.nc.psum_tensor(name, list(shape), dt))

    def emit(self, eng, fn, waits=(), inc=None, inc_n=1):
        for s, _ in waits:
            self.sem(s)
        chain = _CHAIN_SEM.get(eng)
        if chain is not None:
            assert inc is None or inc == chain, f"use chain sem {chain} on {eng}"
            mock = _CountMock()
            fn(mock)
            self.sem(chain)
            self.semc[chain] += mock.n
            self.items[eng].append((tuple(waits), fn, None, 0))
            return self.semc[chain]
        if inc is not None:
            self.sem(inc)
            self.semc[inc] += inc_n
        self.items[eng].append((tuple(waits), fn, inc, inc_n))
        return self.semc[inc] if inc is not None else None

    def c(self, sem):
        self.sem(sem)
        return self.semc[sem]

    def build(self):
        chain_state = {e: {"n": 0, "pend": 0, "last_ins": None} for e in _CHAIN_SEM}
        with self.nc.Block() as block:
            def mk(eng_name):
                def runner(eng):
                    chain = _CHAIN_SEM.get(eng_name)
                    strict = self.strict_chains or eng_name == "gpsimd"
                    if chain is not None:
                        eng_x = _ChainProxy(eng, self.semh[chain],
                                            chain_state[eng_name], strict=strict)
                    else:
                        eng_x = eng
                    hi = {}
                    for waits, fn, inc, inc_n in self.items[eng_name]:
                        for s, v in waits:
                            if hi.get(s, -1) < v:
                                eng.wait_ge(self.semh[s], v)
                                hi[s] = v
                        ins = fn(eng_x)
                        if eng_x is not eng:
                            eng_x._commit()
                        if inc is not None:
                            assert ins is not None, f"emit with inc={inc} returned None"
                            ins.then_inc(self.semh[inc], inc_n)
                return runner
            for e in self.ENGINES:
                if self.items[e]:
                    getattr(block, e)(mk(e))
        for e, st in chain_state.items():
            assert st["n"] == self.semc[_CHAIN_SEM[e]], \
                f"chain count mismatch on {e}: {st['n']} != {self.semc[_CHAIN_SEM[e]]}"
        return self.nc


def common_consts(p):
    cst = {}
    cst["ones_col_f"] = p.sbuf("c_ones_col_f", (128, 1), F32)
    cst["ones_row_f"] = p.sbuf("c_ones_row_f", (1, 128), F32)
    cst["ones_row_b"] = p.sbuf("c_ones_row_b", (1, 128), BF16)
    cst["ones_tb_b"] = p.sbuf("c_ones_tb_b", (1, TB), BF16)
    cst["eps_row"] = p.sbuf("c_eps_row", (1, 1), F32)
    cst["id_f"] = p.sbuf("c_id_f", (128, 128), F32)
    cst["id_b"] = p.sbuf("c_id_b", (128, 128), BF16)

    def _(v):
        v.memset(cst["ones_col_f"][:, :], 1.0)
        v.memset(cst["ones_row_f"][:, :], 1.0)
        v.memset(cst["ones_row_b"][:, :], 1.0)
        v.memset(cst["eps_row"][:, :], LN_EPS)
        return v.memset(cst["ones_tb_b"][:, :], 1.0)
    p.emit("vector", _, inc="dv")

    def _(g):
        g.memset(cst["id_f"][:, :], 1.0)
        g.memset(cst["id_b"][:, :], 1.0)
        g.affine_select(cst["id_f"][:, :], cst["id_f"][:, :], [[-1, 128]], ALU.is_equal,
                        0.0, base=0, channel_multiplier=1)
        return g.affine_select(cst["id_b"][:, :], cst["id_b"][:, :], [[-1, 128]], ALU.is_equal,
                               0.0, base=0, channel_multiplier=1)
    p.emit("gpsimd", _, inc="po")
    cst["ready"] = [("dv", p.c("dv")), ("po", p.c("po"))]
    return cst


def ln_pool(p, ext=None, tag=""):
    ent = (ext.enter_context if ext is not None
           else p.stack.enter_context)
    nc = p.nc
    return {
        "sq": ent(nc.sbuf_tensor(f"lnp_sq{tag}", [128, TB], F32)),
        "srow": ent(nc.sbuf_tensor(f"lnp_sum{tag}", [1, T], F32)),
        "qrow": ent(nc.sbuf_tensor(f"lnp_qsum{tag}", [1, T], F32)),
        "irow": ent(nc.sbuf_tensor(f"lnp_istd{tag}", [1, T], F32)),
    }


def emit_ln_fm(p, cst, pp, lnp, x_tiles, out_tiles, wait_x, gamma_col=None, beta_col=None):
    sq, srow, qrow = lnp["sq"], lnp["srow"], lnp["qrow"]
    mrow, irow = lnp["srow"], lnp["irow"]
    psA, psB = pp["rowA"], pp["rowB"]

    for tb in range(NTB):
        t0, t1 = tb * TB, (tb + 1) * TB
        dv_now = p.c("dv")

        def _(t, t0=t0, t1=t1):
            last = None
            for dc in range(DC):
                last = t.matmul(psA[0:1, :], cst["ones_col_f"][:, :],
                                x_tiles[dc][:, t0:t1], start=(dc == 0), stop=(dc == DC - 1))
            return last
        pe_v = p.emit("tensor", _, waits=list(wait_x) + cst["ready"] + [("dv", dv_now)],
                      inc="pe")

        def _(v, t0=t0, t1=t1):
            return v.tensor_copy(srow[:, t0:t1], psA[0:1, :])
        p.emit("vector", _, waits=[("pe", pe_v)], inc="dv")

        for dc in range(DC):
            def _(s, dc=dc, t0=t0, t1=t1):
                return s.activation(sq[:, :], x_tiles[dc][:, t0:t1], AF.Square)
            ac_v = p.emit("scalar", _, waits=list(wait_x) + [("pe", p.c("pe"))], inc="ac")

            def _(t, dc=dc):
                return t.matmul(psB[0:1, :], cst["ones_col_f"][:, :], sq[:, :],
                                start=(dc == 0), stop=(dc == DC - 1))
            p.emit("tensor", _, waits=[("ac", ac_v), ("dv", p.c("dv"))], inc="pe")

        def _(v, t0=t0, t1=t1):
            return v.tensor_copy(qrow[:, t0:t1], psB[0:1, :])
        p.emit("vector", _, waits=[("pe", p.c("pe"))], inc="dv")

    def _(v):
        v.tensor_scalar_mul(mrow[:, :], srow[:, :], 1.0 / D)
        v.tensor_scalar_mul(qrow[:, :], qrow[:, :], 1.0 / D)
        v.tensor_mul(irow[:, :], mrow[:, :], mrow[:, :])
        return v.tensor_sub(qrow[:, :], qrow[:, :], irow[:, :])
    dv_var = p.emit("vector", _, waits=[("dv", p.c("dv"))], inc="dv")

    def _(s):
        return s.activation(qrow[:, :], qrow[:, :], AF.Sqrt, bias=cst["eps_row"][:, :])
    ac_std = p.emit("scalar", _, waits=[("dv", dv_var)], inc="ac")

    def _(v):
        v.reciprocal(irow[:, :], qrow[:, :])
        return v.tensor_mul(mrow[:, :], mrow[:, :], irow[:, :])
    dv_ist = p.emit("vector", _, waits=[("ac", ac_std)], inc="dv")

    for tb in range(NTB):
        t0, t1 = tb * TB, (tb + 1) * TB
        dv_prev = p.c("dv")

        def _(t, t0=t0, t1=t1):
            t.matmul(pp["bc0"][:, :], cst["ones_row_f"][:, :], irow[0:1, t0:t1],
                     start=True, stop=True)
            return t.matmul(pp["bc1"][:, :], cst["ones_row_f"][:, :], mrow[0:1, t0:t1],
                            start=True, stop=True)
        pe_bc = p.emit("tensor", _, waits=[("dv", dv_prev), ("dv", dv_ist)], inc="pe")

        for dc in range(DC):
            if gamma_col is None:
                def _(v, dc=dc, t0=t0, t1=t1):
                    v.tensor_mul(sq[:, :], x_tiles[dc][:, t0:t1], pp["bc0"][:, :])
                    return v.tensor_sub(out_tiles[dc][:, t0:t1], sq[:, :], pp["bc1"][:, :])
            else:
                def _(v, dc=dc, t0=t0, t1=t1):
                    v.tensor_mul(sq[:, :], x_tiles[dc][:, t0:t1], pp["bc0"][:, :])
                    v.tensor_sub(sq[:, :], sq[:, :], pp["bc1"][:, :])
                    return v.tensor_scalar(out_tiles[dc][:, t0:t1], sq[:, :],
                                           gamma_col[dc], beta_col[dc], ALU.mult, ALU.add)
            p.emit("vector", _, waits=[("pe", pe_bc)], inc="dv")
    return [("dv", p.c("dv"))]


def emit_top2(p, cst, pp, xn_tiles, wt_router, brow_router, rw_tm, wait_xn, n_exp=8):
    u = p.uid()
    pr = p.sbuf(f"rt_pr{u}", (128, n_exp), F32)
    tmp = p.sbuf(f"rt_tmp{u}", (128, n_exp), F32)
    cum = p.sbuf(f"rt_cum{u}", (128, n_exp), F32)
    sel = p.sbuf(f"rt_sel{u}", (128, n_exp), F32)
    sel2 = p.sbuf(f"rt_sel2{u}", (128, n_exp), F32)
    zer = p.sbuf(f"rt_zero{u}", (128, n_exp), F32)
    col = p.sbuf(f"rt_col{u}", (128, 1), F32)
    csum = p.sbuf(f"rt_csum{u}", (128, 1), F32)
    pst = pp["mm0"]

    def _(v):
        return v.memset(zer[:, :], 0.0)
    p.emit("vector", _, inc="dv")

    for tc in range(T // 128):
        c0 = tc * 128
        def _(t, c0=c0):
            last = None
            for dc in range(DC):
                last = t.matmul(pst[:, 0:n_exp], xn_tiles[dc][:, c0:c0 + 128],
                                wt_router[dc][:, :], start=(dc == 0), stop=False)
            if brow_router is not None:
                last = t.matmul(pst[:, 0:n_exp], cst["ones_row_b"][:, :],
                                brow_router[0:1, :], start=False, stop=True)
            else:
                last.stop = True
            return last
        pe_l = p.emit("tensor", _, waits=list(wait_xn) + [("dv", p.c("dv"))], inc="pe")

        def _(v):
            return v.reduce_max(col[:, :], pst[:, 0:n_exp], AX.X, negate=True)
        dv_m = p.emit("vector", _, waits=[("pe", pe_l)], inc="dv")

        def _(s):
            return s.activation(pr[:, :], pst[:, 0:n_exp], AF.Exp, bias=col[:, :],
                                accum_out=csum[:, :])
        ac_e = p.emit("scalar", _, waits=[("dv", dv_m)], inc="ac")

        def _(v, tc=tc):
            v.reciprocal(col[:, :], csum[:, :])
            v.tensor_scalar(pr[:, :], pr[:, :], col[:, :], None, ALU.mult)
            v.reduce_max(col[:, :], pr[:, :], AX.X)
            v.tensor_scalar(tmp[:, :], pr[:, :], col[:, :], None, ALU.is_ge)
            v.tensor_tensor_scan(cum[:, :], tmp[:, :], zer[:, :], 0.0, ALU.add, ALU.add)
            v.tensor_scalar(cum[:, :], cum[:, :], 1.0, None, ALU.is_le)
            v.tensor_mul(sel[:, :], tmp[:, :], cum[:, :])
            v.scalar_tensor_tensor(tmp[:, :], sel[:, :], -1e30, pr[:, :], ALU.mult, ALU.add)
            v.reduce_max(col[:, :], tmp[:, :], AX.X)
            v.tensor_scalar(sel2[:, :], tmp[:, :], col[:, :], None, ALU.is_ge)
            v.tensor_tensor_scan(cum[:, :], sel2[:, :], zer[:, :], 0.0, ALU.add, ALU.add)
            v.tensor_scalar(cum[:, :], cum[:, :], 1.0, None, ALU.is_le)
            v.tensor_mul(sel2[:, :], sel2[:, :], cum[:, :])
            v.tensor_add(sel[:, :], sel[:, :], sel2[:, :])
            v.tensor_mul(pr[:, :], pr[:, :], sel[:, :])
            return v.tensor_copy(rw_tm[:, tc::T // 128], pr[:, :])
        dv_rw = p.emit("vector", _, waits=[("ac", ac_e)], inc="dv")
    return [("dv", p.c("dv"))]


def emit_rw_row(p, cst, pp, rw_tm, rw_row, e, wait_rw, n_exp=8):
    NTC = T // 128
    for tc in range(NTC):
        def _(t, e=e, tc=tc):
            return t.transpose(pp["tp0"][0:1, 0:128], rw_tm[:, e * NTC + tc:e * NTC + tc + 1],
                               cst["id_f"][:, :])
        pe_t = p.emit("tensor", _, waits=list(wait_rw) + [("dv", p.c("dv"))], inc="pe")

        def _(v, tc=tc):
            return v.tensor_copy(rw_row[0:1, tc * 128:(tc + 1) * 128], pp["tp0"][0:1, 0:128])
        p.emit("vector", _, waits=[("pe", pe_t)], inc="dv")
    return [("dv", p.c("dv"))]


def dma_batch(p, eng, pairs, sem, waits=(), transpose=False):
    p.sem(sem)
    n = len(pairs)
    semh = p.semh[sem]

    def fn(eng_obj, pairs=tuple(pairs)):
        for out_ap, in_ap in pairs:
            if transpose:
                eng_obj.dma_start(out_ap, in_ap, transpose=True).then_inc(semh, 16)
            else:
                eng_obj.dma_start(out_ap, in_ap).then_inc(semh, 16)
        return None
    for s_, _ in waits:
        p.sem(s_)
    p.semc[sem] += 16 * n
    p.items[eng].append((tuple(waits), fn, None, 0))
    return p.semc[sem]


class Arena:

    def __init__(self, p, name, cols):
        self.t = p.sbuf(name, (128, cols), F32)
        self.cols = cols
        self.off = 0

    def take(self, shape, dt=F32):
        R, N = shape
        n32 = N if dt == F32 else (N + 1) // 2
        if dt == F32:
            v = self.t[0:R, self.off:self.off + n32]
        else:
            v = self.t[:, self.off:self.off + n32].bitcast(dt)[0:R, 0:N]
        self.off += n32
        assert self.off <= self.cols, f"arena overflow: {self.off} > {self.cols}"
        return v

    def take_cols(self, n32):
        v = self.t[:, self.off:self.off + n32]
        o = self.off
        self.off += n32
        assert self.off <= self.cols, f"arena overflow: {self.off} > {self.cols}"
        return v, o

    def reset(self):
        self.off = 0


def make_psum_pool(p):
    return [p.psum(f"pb{i}", (128, TB)) for i in range(8)]


def pp_roles(pb):
    return {
        "rowA": pb[0][0:1, :], "rowB": pb[1][0:1, :],
        "bc0": pb[2], "bc1": pb[3], "mm0": pb[4], "mm1": pb[5],
        "tp0": pb[6][:, 0:128], "tp1": pb[7][:, 0:128],
    }


def emit_fence(p, dma_sems=()):
    w_all = [("dv", p.c("dv")), ("ac", p.c("ac")), ("po", p.c("po")),
             ("pe", p.c("pe"))] + [(s, p.c(s)) for s in dma_sems]

    def nopf(e):
        return None
    p.emit("vector", nopf, waits=[w for w in w_all if w[0] != "dv"])
    p.emit("scalar", nopf, waits=[w for w in w_all if w[0] != "ac"])
    p.emit("gpsimd", nopf, waits=[w for w in w_all if w[0] != "po"])
    p.emit("tensor", nopf, waits=[w for w in w_all if w[0] != "pe"])
    p.emit("sync", nopf, waits=list(w_all))


def emit_transpose_out(p, cst, pp, x_tiles, d_out, s, stage, wait_x=(), st_sem="st"):
    for tc in range(T // 128):
        sb = stage[tc % 2]
        for dc in range(DC):
            tp = pp["tp0"] if dc % 2 == 0 else pp["tp1"]

            def _(t, tc=tc, dc=dc, tp=tp):
                return t.transpose(tp, x_tiles[dc][:, tc * 128:(tc + 1) * 128],
                                   cst["id_f"][:, :])
            pe_t = p.emit("tensor", _, waits=list(wait_x) + [("dv", p.c("dv")),
                                                             (st_sem, p.c(st_sem))], inc="pe")

            def _(v, dc=dc, sb=sb, tp=tp):
                return v.tensor_copy(sb[:, dc * 128:(dc + 1) * 128], tp)
            p.emit("vector", _, waits=[("pe", pe_t)])
        dma_batch(p, "sync", [(d_out[s, tc * 128:(tc + 1) * 128, :], sb[:, :])],
                  st_sem, waits=[("dv", p.c("dv"))])


def emit_l1ffn(p, cst, pb, A, dT, pfx="f1"):
    NE = 8
    d_x3 = dT["x3"]; d_wet = dT["WeT"]; d_be = dT["be"]
    d_wrt = dT["WrT"]; d_br = dT["br"]; d_x4 = dT["x4"]
    if True:
        pp = pp_roles(pb)
        lnp = {"sq": A.take((128, TB)), "srow": A.take((1, T)),
               "qrow": A.take((1, T)), "irow": A.take((1, T))}
        x_tiles = [A.take((128, T)) for dc in range(DC)]
        xn = [A.take((128, T), BF16) for dc in range(DC)]
        wrt = [A.take((128, NE), BF16) for dc in range(DC)]
        br_row = A.take((1, NE), BF16)
        be_row2 = [A.take((1, D), BF16) for b in range(2)]
        rw_tm = A.take((128, NE * (T // 128)))
        rw_row = A.take((1, T))
        rwbc = A.take((128, T))
        wt = [[A.take((128, D), BF16) for dc in range(DC)] for b in range(2)]
        stage = [A.take((128, D)) for b in range(2)]
        evtmp = A.take((128, TB))

        gl0 = dma_batch(p, "gpsimd",
                        [(wrt[dc][:, :], d_wrt[dc * 128:(dc + 1) * 128, :])
                         for dc in range(DC)]
                        + [(br_row[:, :], d_br[:, :])],
                        pfx + "gsm")

        pe_after_expert = {}
        for s in range(NSEQ):
            ld_v = dma_batch(p, "sync",
                             [(x_tiles[dc][:, :], d_x3[s, dc * 128:(dc + 1) * 128, :])
                              for dc in range(DC)],
                             pfx + "ld", waits=[("dv", p.c("dv")), ("pe", p.c("pe")),
                                          (pfx + "st", p.c(pfx + "st"))])
            wait_x = [(pfx + "ld", ld_v)]

            wait_xn = emit_ln_fm(p, cst, pp, lnp, x_tiles, xn, wait_x)
            wait_rw = emit_top2(p, cst, pp, xn, wrt, br_row, rw_tm,
                                wait_xn + [(pfx + "gsm", gl0)], n_exp=NE)

            for e in range(NE):
                buf = wt[e % 2]
                k = (s, e)
                prev2 = pe_after_expert.get((s, e - 2),
                                            pe_after_expert.get((s - 1, NE - 2 + e), 0))
                be_row = be_row2[e % 2]
                gl = dma_batch(p, "gpsimd",
                               [(buf[dc][:, :], d_wet[e, dc * 128:(dc + 1) * 128, :])
                                for dc in range(DC)]
                               + [(be_row[:, :], d_be[e:e + 1, :])],
                               pfx + f"gw{e % 2}", waits=[("pe", prev2)])
                wait_row = emit_rw_row(p, cst, pp, rw_tm, rw_row, e, wait_rw, n_exp=NE)
                for tb in range(NTB):
                    t0, t1 = tb * TB, (tb + 1) * TB

                    def _(t, t0=t0, t1=t1):
                        return t.matmul(pp["bc0"][:, :], cst["ones_row_f"][:, :],
                                        rw_row[0:1, t0:t1], start=True, stop=True)
                    pe_b = p.emit("tensor", _,
                                  waits=list(wait_row) + [("dv", p.c("dv"))], inc="pe")

                    def _(v, t0=t0, t1=t1):
                        return v.tensor_copy(rwbc[:, t0:t1], pp["bc0"][:, :])
                    p.emit("vector", _, waits=[("pe", pe_b)], inc="dv")
                dv_rwbc = p.c("dv")

                for oc in range(DC):
                    for tb in range(NTB):
                        t0, t1 = tb * TB, (tb + 1) * TB
                        ps = pp["mm0"] if (oc * NTB + tb) % 2 == 0 else pp["mm1"]

                        def mmfn(t, oc=oc, t0=t0, t1=t1, ps=ps, buf=buf, be_row=be_row):
                            t.matmul(ps[:, :], be_row[0:1, oc * 128:(oc + 1) * 128],
                                     cst["ones_tb_b"][:, :], start=True, stop=False)
                            last = None
                            for dc in range(DC):
                                last = t.matmul(ps[:, :], buf[dc][:, oc * 128:(oc + 1) * 128],
                                                xn[dc][:, t0:t1],
                                                start=False, stop=(dc == DC - 1))
                            return last
                        pe_m = p.emit("tensor", mmfn,
                                      waits=[(pfx + f"gw{e % 2}", gl), ("dv", p.c("dv")),
                                             ("ac", p.c("ac")), ("dv", dv_rwbc)], inc="pe")

                        def evfn(v, oc=oc, t0=t0, t1=t1, ps=ps):
                            v.tensor_mul(evtmp[:, :], ps[:, :], rwbc[:, t0:t1])
                            return v.tensor_add(x_tiles[oc][:, t0:t1],
                                                x_tiles[oc][:, t0:t1], evtmp[:, :])
                        p.emit("vector", evfn, waits=[("pe", pe_m)], inc="dv")
                pe_after_expert[k] = p.c("pe")

            emit_transpose_out(p, cst, pp, x_tiles, d_x4, s, stage, st_sem=pfx + 'st')

    return pfx + "st"


def build_l1ffn_launch():
    nc = bass.Bass("TRN2", target_bir_lowering=False)
    NE = 8
    dT = {
        "x3": nc.dram_tensor("x3", [NSEQ, D, T], F32, kind="ExternalInput"),
        "WeT": nc.dram_tensor("WeT", [NE, D, D], F32, kind="ExternalInput"),
        "be": nc.dram_tensor("be", [NE, D], F32, kind="ExternalInput"),
        "WrT": nc.dram_tensor("WrT", [D, NE], F32, kind="ExternalInput"),
        "br": nc.dram_tensor("br", [1, NE], F32, kind="ExternalInput"),
        "x4": nc.dram_tensor("x4", [NSEQ, T, D], F32, kind="ExternalOutput"),
    }
    p = Prog(nc)
    with p.stack:
        cst = common_consts(p)
        pb = make_psum_pool(p)
        A = Arena(p, "arena", 52000)
        st = emit_l1ffn(p, cst, pb, A, dT)
        p.emit("sync", lambda y: None, waits=[(st, p.c(st))])
        p.build()
    return nc


HID = 2816
HHALF = HID // 2
MCH = HHALF // 128


def emit_l0ffn(p, cst, pb, A, dT, pfx="f0"):
    NE = 8
    d_x1 = dT["x1"]; d_gwt = dT["gwT"]; d_w1 = dT["w1"]
    d_w2 = dT["w2"]; d_w3 = dT["w3"]; d_x2 = dT["x2"]
    if True:
        pp = pp_roles(pb)
        x_tiles = [A.take((128, T)) for dc in range(DC)]
        xn = [A.take((128, T), BF16) for dc in range(DC)]
        gwt = [A.take((128, NE), BF16) for dc in range(DC)]
        zrow = A.take((1, NE), BF16)
        rw_tm = A.take((128, NE * (T // 128)))
        w1b = [A.take((128, HHALF), BF16) for dc in range(DC)]
        w3b = [A.take((128, HHALF), BF16) for dc in range(DC)]
        w2b = [A.take((128, D), BF16) for k in range(MCH)]
        TBF, NTBF = 512, T // 512
        uscr, _uo = A.take_cols(7424)
        lnp = {"sq": uscr[:, 0:512], "srow": uscr[0:1, 512:2560],
               "qrow": uscr[0:1, 2560:4608], "irow": uscr[0:1, 4608:6656]}
        rw_row = uscr[:, 0:1024].bitcast(BF16)[0:1, 0:T]
        rwbc = uscr[:, 1024:2048].bitcast(BF16)
        xe = [uscr[:, 2048 + dc * 256:2048 + (dc + 1) * 256].bitcast(BF16)
              for dc in range(DC)]
        hb = uscr[:, 4096:6912].bitcast(BF16)
        hs = uscr[:, 6912:7424]

        def _(v):
            return v.memset(zrow[:, :], 0.0)
        p.emit("vector", _)

        gl0 = dma_batch(p, "gpsimd",
                        [(gwt[dc][:, :], d_gwt[dc * 128:(dc + 1) * 128, :])
                         for dc in range(DC)], pfx + "gsm")

        for s in range(NSEQ):
            ld_v = dma_batch(p, "sync",
                             [(x_tiles[dc][:, :], d_x1[s, dc * 128:(dc + 1) * 128, :])
                              for dc in range(DC)],
                             pfx + "ld", waits=[("dv", p.c("dv")), ("pe", p.c("pe")),
                                          (pfx + "st", p.c(pfx + "st"))])
            wait_x = [(pfx + "ld", ld_v)]

            wait_xn = emit_ln_fm(p, cst, pp, lnp, x_tiles, xn, wait_x)
            wait_rw = emit_top2(p, cst, pp, xn, gwt, zrow, rw_tm,
                                wait_xn + [(pfx + "gsm", gl0)], n_exp=NE)

            for e in range(NE):
                wait_row = emit_rw_row(p, cst, pp, rw_tm, rw_row, e, wait_rw, n_exp=NE)
                for tb in range(NTB):
                    t0, t1 = tb * TB, (tb + 1) * TB

                    def _(t, t0=t0, t1=t1):
                        return t.matmul(pp["bc0"][:, :], cst["ones_row_b"][:, :],
                                        rw_row[0:1, t0:t1], start=True, stop=True)
                    pe_b = p.emit("tensor", _,
                                  waits=list(wait_row) + [("dv", p.c("dv"))], inc="pe")

                    def _(v, t0=t0, t1=t1):
                        return v.tensor_copy(rwbc[:, t0:t1], pp["bc0"][:, :])
                    p.emit("vector", _, waits=[("pe", pe_b)], inc="dv")
                dv_rwbc = p.c("dv")

                for half in range(2):
                    h0 = half * HHALF
                    gw = dma_batch(
                        p, "gpsimd",
                        [(w1b[dc][:, :], d_w1[e, dc * 128:(dc + 1) * 128, h0:h0 + HHALF])
                         for dc in range(DC)]
                        + [(w3b[dc][:, :], d_w3[e, dc * 128:(dc + 1) * 128, h0:h0 + HHALF])
                           for dc in range(DC)]
                        + [(w2b[k][:, :], d_w2[e, h0 + k * 128:h0 + (k + 1) * 128, :])
                           for k in range(MCH)],
                        pfx + "gwl", waits=[("pe", p.c("pe"))])

                    for tb in range(NTBF):
                        t0, t1 = tb * TBF, (tb + 1) * TBF
                        def _(v, t0=t0, t1=t1):
                            last = None
                            for dc in range(DC):
                                last = v.tensor_mul(xe[dc][:, :], xn[dc][:, t0:t1],
                                                    rwbc[:, t0:t1])
                            return last
                        dv_xe = p.emit("vector", _, waits=[("dv", dv_rwbc),
                                                           ("pe", p.c("pe"))])

                        for mc in range(MCH):
                            m0 = mc * 128

                            def _(t, m0=m0):
                                for dc in range(DC):
                                    t.matmul(pp["mm0"][:, 0:TBF], w1b[dc][:, m0:m0 + 128],
                                             xe[dc][:, :], start=(dc == 0),
                                             stop=(dc == DC - 1))
                                last = None
                                for dc in range(DC):
                                    last = t.matmul(pp["mm1"][:, 0:TBF], w3b[dc][:, m0:m0 + 128],
                                                    xe[dc][:, :], start=(dc == 0),
                                                    stop=(dc == DC - 1))
                                return last
                            pe_h = p.emit("tensor", _,
                                          waits=[(pfx + "gwl", gw), ("dv", dv_xe),
                                                 ("ac", p.c("ac")), ("dv", p.c("dv"))],
                                          inc="pe")

                            def _(s_, mc=mc):
                                return s_.activation(hs[:, :], pp["mm0"][:, 0:TBF], AF.Silu)
                            ac_s = p.emit("scalar", _, waits=[("pe", pe_h)])

                            def _(v, mc=mc):
                                return v.tensor_mul(hb[:, mc * TBF:(mc + 1) * TBF],
                                                    hs[:, :], pp["mm1"][:, 0:TBF])
                            p.emit("vector", _, waits=[("ac", ac_s), ("pe", pe_h)])
                        dv_h = p.c("dv")

                        for oc in range(DC):
                            ps = pp["mm0"] if oc % 2 == 0 else pp["mm1"]

                            def _(t, oc=oc, ps=ps):
                                last = None
                                for k in range(MCH):
                                    last = t.matmul(ps[:, 0:TBF], w2b[k][:, oc * 128:(oc + 1) * 128],
                                                    hb[:, k * TBF:(k + 1) * TBF],
                                                    start=(k == 0), stop=(k == MCH - 1))
                                return last
                            pe_y = p.emit("tensor", _,
                                          waits=[("dv", dv_h), ("dv", p.c("dv")),
                                                 ("ac", p.c("ac"))], inc="pe")

                            def _(v, oc=oc, t0=t0, t1=t1, ps=ps):
                                return v.tensor_add(x_tiles[oc][:, t0:t1],
                                                    x_tiles[oc][:, t0:t1], ps[:, 0:TBF])
                            p.emit("vector", _, waits=[("pe", pe_y)])

            dma_batch(p, "sync",
                      [(d_x2[s, dc * 128:(dc + 1) * 128, :], x_tiles[dc][:, :])
                       for dc in range(DC)],
                      pfx + "st", waits=[("dv", p.c("dv"))])

    return pfx + "st"


def build_l0ffn_launch():
    nc = bass.Bass("TRN2", target_bir_lowering=False)
    NE = 8
    dT = {
        "x1": nc.dram_tensor("x1", [NSEQ, D, T], F32, kind="ExternalInput"),
        "gwT": nc.dram_tensor("gwT", [D, NE], F32, kind="ExternalInput"),
        "w1": nc.dram_tensor("w1", [NE, D, HID], F32, kind="ExternalInput"),
        "w2": nc.dram_tensor("w2", [NE, HID, D], F32, kind="ExternalInput"),
        "w3": nc.dram_tensor("w3", [NE, D, HID], F32, kind="ExternalInput"),
        "x2": nc.dram_tensor("x2", [NSEQ, D, T], F32, kind="ExternalOutput"),
    }
    p = Prog(nc)
    with p.stack:
        cst = common_consts(p)
        pb = make_psum_pool(p)
        A = Arena(p, "arena", 52000)
        st = emit_l0ffn(p, cst, pb, A, dT)
        p.emit("sync", lambda y: None, waits=[(st, p.c(st))])
        p.build()
    return nc


def emit_l0attn(p, cst, pb, A, dT, pfx="a0"):
    NH, HD = 16, 64
    d_x = dT["x"]; d_wqkv = dT["wqkv"]; d_bq = dT["bq"]
    d_wo = dT["wo"]; d_bo = dT["bo"]; d_xfm = dT["xfm_scr"]; d_x1 = dT["x1"]
    if True:
        pp = pp_roles(pb)
        tpb = [pb[6][:, 128:192].bitcast(BF16), pb[7][:, 128:192].bitcast(BF16)]
        po_ = pp["mm1"][:, 0:128]

        uscrA, _oA = A.take_cols(16384)
        xfm_t = [uscrA[:, dc * 2048:(dc + 1) * 2048] for dc in range(DC)]
        V_tm = uscrA[:, 0:8192].bitcast(BF16)
        O_fm = [uscrA[:, 8192 + dc * 1024:8192 + (dc + 1) * 1024].bitcast(BF16)
                for dc in range(DC)]
        uscrB, _oB = A.take_cols(6656)
        x_tm = [uscrB[:, 0:1024], uscrB[:, 1024:2048]]
        lnp = {"sq": uscrB[:, 0:512], "srow": uscrB[0:1, 512:2560],
               "qrow": uscrB[0:1, 2560:4608], "irow": uscrB[0:1, 4608:6656]}
        score_sb = uscrB[:, 0:2048]
        dot2 = uscrB[:, 2048:2560]
        rcp = uscrB[:, 2560:3072]
        Pb = uscrB[:, 3072:4096].bitcast(BF16)
        qnb = uscrB[:, 4608:5632].bitcast(BF16)[0:1, 0:T]
        knb = uscrB[:, 5632:6656].bitcast(BF16)[0:1, 0:T]
        uscrC, _oC = A.take_cols(4096)
        wv = [uscrC[:, dc * 512:(dc + 1) * 512].bitcast(BF16) for dc in range(DC)]
        PT = uscrC[:, 0:1088].bitcast(BF16)
        Qsq = uscrC[0:64, 1088:3136]

        xn = [A.take((128, T), BF16) for dc in range(DC)]
        QA = A.take((64, T), BF16); QB = A.take((64, T), BF16)
        MA = A.take((64, T), BF16); MB = A.take((64, T), BF16)
        KA = A.take((64, T), BF16); KB = A.take((64, T), BF16)
        wqs = [A.take((128, DC * 128), BF16) for b in range(2)]
        wks = [A.take((128, DC * 128), BF16) for b in range(2)]
        wo = [A.take((128, D), BF16) for dc in range(DC)]
        bqv_row = A.take((1, D), BF16)
        bq_cols = A.take((128, 2 * NH))
        bo_row = A.take((1, D), BF16)
        mcol = A.take((128, 1))
        lcol = A.take((128, 1))
        b2col = A.take((128, 1))
        xchunk = [A.take((128, TB)) for b in range(2)]
        obuf = [A.take((128, TB)) for b in range(2)]

        maskreg = {}

        def _(g):
            import struct
            r = g.alloc_register("maskfill")
            g.reg_mov(r, struct.unpack('<i', struct.pack('<f', -1e9))[0])
            maskreg["r"] = r
            return None
        p.emit("gpsimd", _)

        gsm = dma_batch(p, "gpsimd",
                        [(bqv_row[:, :], d_bq[0:1, 2 * D:3 * D]),
                         (bo_row[:, :], d_bo[0:1, :])]
                        + [(bq_cols[:, h:h + 1],
                            d_bq[0:1, h * 128:h * 128 + 128].rearrange("o i -> i o"))
                           for h in range(8)]
                        + [(bq_cols[:, 16 + h:17 + h],
                            d_bq[0:1, D + h * 128:D + h * 128 + 128].rearrange("o i -> i o"))
                           for h in range(8)]
                        + [(wo[dc][:, :], d_wo[dc * 128:(dc + 1) * 128, :])
                           for dc in range(DC)],
                        pfx + "gsm")

        for s in range(NSEQ):
            pe_pre = p.c("pe")
            dv_pre = p.c("dv")
            for tc in range(T // 128):
                ld_v = dma_batch(p, "sync",
                                 [(x_tm[tc % 2][:, :], d_x[s, tc * 128:(tc + 1) * 128, :])],
                                 pfx + "ld", waits=[("pe", p.c("pe")), ("dv", p.c("dv"))])
                for dc in range(DC):
                    tp = pp["bc0"][:, 0:128] if dc % 2 == 0 else pp["bc1"][:, 0:128]

                    def _(t, tc=tc, dc=dc, tp=tp):
                        return t.transpose(tp, x_tm[tc % 2][:, dc * 128:(dc + 1) * 128],
                                           cst["id_f"][:, :])
                    pe_t = p.emit("tensor", _, waits=[(pfx + "ld", ld_v), ("dv", p.c("dv"))]
                                  + cst["ready"], inc="pe")

                    def _(v, tc=tc, dc=dc, tp=tp):
                        return v.tensor_copy(xfm_t[dc][:, tc * 128:(tc + 1) * 128], tp)
                    p.emit("vector", _, waits=[("pe", pe_t)])
            sx = dma_batch(p, "sync",
                           [(d_xfm[s, dc * 128:(dc + 1) * 128, :], xfm_t[dc][:, :])
                            for dc in range(DC)],
                           pfx + "sx", waits=[("dv", p.c("dv"))])
            wait_xn = emit_ln_fm(p, cst, pp, lnp, xfm_t, xn, [("dv", p.c("dv"))])

            gv = dma_batch(p, "gpsimd",
                           [(wv[dc][:, :], d_wqkv[dc * 128:(dc + 1) * 128, 2 * D:3 * D])
                            for dc in range(DC)],
                           pfx + "gv", waits=[("pe", p.c("pe")), ("dv", p.c("dv")),
                                        (pfx + "sx", sx)])
            for tc in range(T // 128):
                for vh in range(2):
                    ps = pp["mm0"] if vh == 0 else pp["mm1"]

                    def _(t, tc=tc, vh=vh, ps=ps):
                        t.matmul(ps[:, :], cst["ones_row_b"][:, :],
                                 bqv_row[0:1, vh * TB:(vh + 1) * TB], start=True, stop=False)
                        last = None
                        for dc in range(DC):
                            last = t.matmul(ps[:, :], xn[dc][:, tc * 128:(tc + 1) * 128],
                                            wv[dc][:, vh * TB:(vh + 1) * TB],
                                            start=False, stop=(dc == DC - 1))
                        return last
                    pe_v = p.emit("tensor", _,
                                  waits=list(wait_xn) + [(pfx + "gv", gv), (pfx + "gsm", gsm),
                                                         ("dv", p.c("dv"))], inc="pe")

                    def _(v, tc=tc, vh=vh, ps=ps):
                        return v.tensor_copy(V_tm[:, tc * 1024 + vh * TB:tc * 1024 + (vh + 1) * TB],
                                             ps[:, :])
                    p.emit("vector", _, waits=[("pe", pe_v)])
            dv_vdone = p.c("dv")

            for hp in range(NH // 2):
                par = hp % 2
                gq = dma_batch(
                    p, "gpsimd",
                    [(wqs[par][:, dc * 128:(dc + 1) * 128],
                      d_wqkv[dc * 128:(dc + 1) * 128, hp * 128:(hp + 1) * 128])
                     for dc in range(DC)]
                    + [(wks[par][:, dc * 128:(dc + 1) * 128],
                        d_wqkv[dc * 128:(dc + 1) * 128, D + hp * 128:D + (hp + 1) * 128])
                       for dc in range(DC)],
                    pfx + f"gq{par}", waits=[("pe", p.c("pe"))])
                for tb in range(NTB):
                    t0, t1 = tb * TB, (tb + 1) * TB

                    def _(t, par=par, t0=t0, t1=t1):
                        for dc in range(DC):
                            t.matmul(pp["mm0"][:, :], wqs[par][:, dc * 128:(dc + 1) * 128],
                                     xn[dc][:, t0:t1], start=(dc == 0), stop=(dc == DC - 1))
                        last = None
                        for dc in range(DC):
                            last = t.matmul(pp["mm1"][:, :], wks[par][:, dc * 128:(dc + 1) * 128],
                                            xn[dc][:, t0:t1], start=(dc == 0),
                                            stop=(dc == DC - 1))
                        return last
                    pe_qk = p.emit("tensor", _,
                                   waits=[(pfx + f"gq{par}", gq), ("dv", p.c("dv")),
                                          ("ac", p.c("ac"))], inc="pe")

                    def _(v, hp=hp, t0=t0, t1=t1):
                        qc0 = bq_cols[0:64, hp:hp + 1]
                        qc1 = bq_cols[64:128, hp:hp + 1]
                        kc0 = bq_cols[0:64, 16 + hp:17 + hp]
                        kc1 = bq_cols[64:128, 16 + hp:17 + hp]
                        v.tensor_scalar(QA[:, t0:t1], pp["mm0"][0:64, :], qc0, None, ALU.add)
                        v.tensor_scalar(QB[:, t0:t1], pp["mm0"][64:128, :], qc1, None, ALU.add)
                        v.tensor_scalar(MA[:, t0:t1], pp["mm0"][0:64, :], qc0, -2.0,
                                        ALU.add, ALU.mult)
                        v.tensor_scalar(MB[:, t0:t1], pp["mm0"][64:128, :], qc1, -2.0,
                                        ALU.add, ALU.mult)
                        v.tensor_scalar(KA[:, t0:t1], pp["mm1"][0:64, :], kc0, None, ALU.add)
                        return v.tensor_scalar(KB[:, t0:t1], pp["mm1"][64:128, :], kc1,
                                               None, ALU.add)
                    p.emit("vector", _, waits=[("pe", pe_qk)])
                dv_qk = p.c("dv")

                for hh in range(2):
                    h = 2 * hp + hh
                    Qh = QA if hh == 0 else QB
                    Mh = MA if hh == 0 else MB
                    Kh = KA if hh == 0 else KB

                    def _(s_, Qh=Qh):
                        return s_.activation(Qsq[:, :], Qh[:, :], AF.Square)
                    ac_q = p.emit("scalar", _, waits=[("dv", dv_qk), ("pe", p.c("pe"))])
                    for tb in range(NTB):
                        def _(t, tb=tb):
                            return t.matmul(pp["rowA"][:, :], cst["ones_col_f"][0:64, :],
                                            Qsq[:, tb * TB:(tb + 1) * TB],
                                            start=True, stop=True)
                        pe_q = p.emit("tensor", _, waits=[("ac", ac_q), ("dv", p.c("dv"))],
                                      inc="pe")

                        def _(v, tb=tb):
                            return v.tensor_copy(qnb[0:1, tb * TB:(tb + 1) * TB],
                                                 pp["rowA"][:, :])
                        p.emit("vector", _, waits=[("pe", pe_q)])

                    def _(s_, Kh=Kh):
                        return s_.activation(Qsq[:, :], Kh[:, :], AF.Square)
                    ac_k = p.emit("scalar", _, waits=[("pe", p.c("pe")), ("dv", p.c("dv"))])
                    for tb in range(NTB):
                        def _(t, tb=tb):
                            return t.matmul(pp["rowB"][:, :], cst["ones_col_f"][0:64, :],
                                            Qsq[:, tb * TB:(tb + 1) * TB],
                                            start=True, stop=True)
                        pe_k = p.emit("tensor", _, waits=[("ac", ac_k), ("dv", p.c("dv"))],
                                      inc="pe")

                        def _(v, tb=tb):
                            return v.tensor_scalar(knb[0:1, tb * TB:(tb + 1) * TB],
                                                   pp["rowB"][:, :], 1e-6, None, ALU.add)
                        p.emit("vector", _, waits=[("pe", pe_k)])
                    dv_rows = p.c("dv")

                    for tq in range(T // 128):
                        q0 = tq * 128
                        ext = (tq + 1) * 128
                        nblk = (ext + TB - 1) // TB
                        dv_blk0 = p.c("dv")
                        ac_blk0 = p.c("ac")
                        for blk in range(nblk):
                            b0 = blk * TB
                            bw = min(TB, ext - b0)
                            psd = pp["mm0"] if blk % 2 == 0 else pp["mm1"]
                            psD = pp["bc0"] if blk % 2 == 0 else pp["bc1"]

                            def _(t, Qh=Qh, Mh=Mh, Kh=Kh, q0=q0, b0=b0, bw=bw,
                                  psd=psd, psD=psD, tq=tq):
                                t.matmul(psd[:, 0:bw], Qh[:, q0:q0 + 128],
                                         Kh[:, b0:b0 + bw], start=True, stop=True)
                                t.matmul(psD[:, 0:bw], qnb[0:1, q0:q0 + 128],
                                         cst["ones_tb_b"][0:1, 0:bw], start=True, stop=False)
                                t.matmul(psD[:, 0:bw], cst["ones_row_b"][:, :],
                                         knb[0:1, b0:b0 + bw], start=False, stop=False)
                                return t.matmul(psD[:, 0:bw], Mh[:, q0:q0 + 128],
                                                Kh[:, b0:b0 + bw], start=False, stop=True)
                            pe_s = p.emit("tensor", _,
                                          waits=[("dv", dv_blk0 if blk < 2 else p.c("dv")),
                                                 ("ac", ac_blk0 if blk < 2 else p.c("ac")),
                                                 ("dv", dv_rows)], inc="pe")

                            def _(s_, b0=b0, bw=bw, psd=psd):
                                return s_.activation(dot2[:, 0:bw], psd[:, 0:bw], AF.Square)
                            p.emit("scalar", _, waits=[("pe", pe_s), ("dv", p.c("dv"))])

                            def _(v, b0=b0, bw=bw, psD=psD):
                                return v.reciprocal(rcp[:, 0:bw], psD[:, 0:bw])
                            p.emit("vector", _, waits=[("pe", pe_s)])

                            def _(v, b0=b0, bw=bw):
                                return v.tensor_mul(score_sb[:, b0:b0 + bw],
                                                    dot2[:, 0:bw], rcp[:, 0:bw])
                            p.emit("vector", _, waits=[("ac", p.c("ac"))])
                        dv_sc = p.c("dv")

                        def _(g, q0=q0):
                            return g.affine_select(score_sb[:, q0:q0 + 128],
                                                   score_sb[:, q0:q0 + 128],
                                                   [[-1, 128]], ALU.is_ge,
                                                   maskreg.get("r", 0.0),
                                                   base=0, channel_multiplier=1)
                        po_m = p.emit("gpsimd", _, waits=[("dv", dv_sc)])

                        def _(v, ext=ext):
                            return v.reduce_max(mcol[:, :], score_sb[:, 0:ext], AX.X,
                                                negate=True)
                        p.emit("vector", _, waits=[("po", po_m)])

                        def _(s_, ext=ext):
                            return s_.activation(Pb[:, 0:ext], score_sb[:, 0:ext], AF.Exp,
                                                 bias=mcol[:, :], accum_out=lcol[:, :])
                        ac_e1 = p.emit("scalar", _, waits=[("dv", p.c("dv"))])

                        def _(s_):
                            return s_.activation(b2col[:, :], lcol[:, :], AF.Ln)
                        p.emit("scalar", _)

                        def _(v):
                            return v.tensor_sub(b2col[:, :], mcol[:, :], b2col[:, :])
                        dv_b2 = p.emit("vector", _, waits=[("ac", p.c("ac"))])

                        def _(s_, ext=ext):
                            return s_.activation(Pb[:, 0:ext], score_sb[:, 0:ext], AF.Exp,
                                                 bias=b2col[:, :])
                        ac_e2 = p.emit("scalar", _, waits=[("dv", dv_b2)])

                        pt_v = dma_batch(p, "scalar",
                                         [(PT[:, kc * 128:(kc + 1) * 128],
                                           Pb[:, kc * 128:(kc + 1) * 128])
                                          for kc in range(tq + 1)],
                                         pfx + "pt", waits=[("ac", ac_e2),
                                                            ("pe", p.c("pe"))],
                                         transpose=True)

                        def _(t, h=h, hh=hh, tq=tq):
                            last = None
                            for kc in range(tq + 1):
                                last = t.matmul(po_[64 * hh:64 * hh + 64, :],
                                                V_tm[:, kc * 1024 + 64 * h:
                                                     kc * 1024 + 64 * h + 64],
                                                PT[:, kc * 128:(kc + 1) * 128],
                                                start=(kc == 0), stop=(kc == tq))
                            return last
                        pe_av = p.emit("tensor", _, waits=[(pfx + "pt", pt_v)], inc="pe")

                        def _(v, hp=hp, hh=hh, tq=tq):
                            return v.tensor_copy(
                                O_fm[hp][64 * hh:64 * hh + 64, tq * 128:(tq + 1) * 128],
                                po_[64 * hh:64 * hh + 64, :])
                        p.emit("vector", _, waits=[("pe", pe_av)])

            for dcO in range(DC):
                for tb in range(NTB):
                    t0, t1 = tb * TB, (tb + 1) * TB
                    ps = pp["mm0"] if (dcO * NTB + tb) % 2 == 0 else pp["mm1"]
                    par = (dcO * NTB + tb) % 2
                    xw = dma_batch(p, "sync",
                                   [(xchunk[par][:, :], d_xfm[s, dcO * 128:(dcO + 1) * 128,
                                                              t0:t1])],
                                   pfx + "lxc", waits=[("dv", p.c("dv"))])

                    def _(t, dcO=dcO, t0=t0, t1=t1, ps=ps):
                        t.matmul(ps[:, :], bo_row[0:1, dcO * 128:(dcO + 1) * 128],
                                 cst["ones_tb_b"][:, :], start=True, stop=False)
                        last = None
                        for dc in range(DC):
                            last = t.matmul(ps[:, :], wo[dc][:, dcO * 128:(dcO + 1) * 128],
                                            O_fm[dc][:, t0:t1], start=False,
                                            stop=(dc == DC - 1))
                        return last
                    pe_o = p.emit("tensor", _, waits=[("dv", p.c("dv")), (pfx + "gsm", gsm),
                                                      ("ac", p.c("ac"))], inc="pe")

                    def _(v, par=par, ps=ps):
                        return v.tensor_add(obuf[par][:, :], xchunk[par][:, :], ps[:, :])
                    p.emit("vector", _, waits=[("pe", pe_o), (pfx + "lxc", xw),
                                               (pfx + "st", p.c(pfx + "st"))])

                    dma_batch(p, "sync",
                              [(d_x1[s, dcO * 128:(dcO + 1) * 128, t0:t1], obuf[par][:, :])],
                              pfx + "st", waits=[("dv", p.c("dv"))])

    return pfx + "st"


def build_l0attn_launch():
    nc = bass.Bass("TRN2", target_bir_lowering=False)
    dT = {
        "x": nc.dram_tensor("x", [NSEQ, T, D], F32, kind="ExternalInput"),
        "wqkv": nc.dram_tensor("wqkv", [D, 3 * D], F32, kind="ExternalInput"),
        "bq": nc.dram_tensor("bq", [1, 3 * D], F32, kind="ExternalInput"),
        "wo": nc.dram_tensor("wo", [D, D], F32, kind="ExternalInput"),
        "bo": nc.dram_tensor("bo", [1, D], F32, kind="ExternalInput"),
        "xfm_scr": nc.dram_tensor("xfm_scr", [NSEQ, D, T], F32),
        "x1": nc.dram_tensor("x1", [NSEQ, D, T], F32, kind="ExternalOutput"),
    }
    p = Prog(nc)
    with p.stack:
        cst = common_consts(p)
        pb = make_psum_pool(p)
        A = Arena(p, "arena", 52000)
        st = emit_l0attn(p, cst, pb, A, dT)
        p.emit("sync", lambda y: None, waits=[(st, p.c(st))])
        p.build()
    return nc


TC = 256
NCH = T // TC
NBLK = 4
EC = 16
NST = 16


def emit_ln_stats(p, cst, pp, lnp, x_tiles, wait_x):
    sq, srow, qrow = lnp["sq"], lnp["srow"], lnp["qrow"]
    mrow, irow = lnp["srow"], lnp["irow"]
    psA, psB = pp["rowA"], pp["rowB"]
    for tb in range(NTB):
        t0, t1 = tb * TB, (tb + 1) * TB
        dv_now = p.c("dv")

        def _(t, t0=t0, t1=t1):
            last = None
            for dc in range(DC):
                last = t.matmul(psA[0:1, :], cst["ones_col_f"][:, :],
                                x_tiles[dc][:, t0:t1], start=(dc == 0), stop=(dc == DC - 1))
            return last
        pe_v = p.emit("tensor", _, waits=list(wait_x) + cst["ready"] + [("dv", dv_now)],
                      inc="pe")

        def _(v, t0=t0, t1=t1):
            return v.tensor_copy(srow[:, t0:t1], psA[0:1, :])
        p.emit("vector", _, waits=[("pe", pe_v)])

        for dc in range(DC):
            def _(s, dc=dc, t0=t0, t1=t1):
                return s.activation(sq[:, :], x_tiles[dc][:, t0:t1], AF.Square)
            ac_v = p.emit("scalar", _, waits=list(wait_x) + [("pe", p.c("pe"))])

            def _(t, dc=dc):
                return t.matmul(psB[0:1, :], cst["ones_col_f"][:, :], sq[:, :],
                                start=(dc == 0), stop=(dc == DC - 1))
            p.emit("tensor", _, waits=[("ac", ac_v), ("dv", p.c("dv"))], inc="pe")

        def _(v, t0=t0, t1=t1):
            return v.tensor_copy(qrow[:, t0:t1], psB[0:1, :])
        p.emit("vector", _, waits=[("pe", p.c("pe"))])

    def _(v):
        v.tensor_scalar_mul(mrow[:, :], srow[:, :], 1.0 / D)
        v.tensor_scalar_mul(qrow[:, :], qrow[:, :], 1.0 / D)
        v.tensor_mul(irow[:, :], mrow[:, :], mrow[:, :])
        return v.tensor_sub(qrow[:, :], qrow[:, :], irow[:, :])
    dv_var = p.emit("vector", _, waits=[("dv", p.c("dv"))])

    def _(s):
        return s.activation(qrow[:, :], qrow[:, :], AF.Sqrt, bias=cst["eps_row"][:, :])
    ac_std = p.emit("scalar", _, waits=[("dv", dv_var)])

    def _(v):
        v.reciprocal(irow[:, :], qrow[:, :])
        return v.tensor_mul(mrow[:, :], mrow[:, :], irow[:, :])
    dv_i = p.emit("vector", _, waits=[("ac", ac_std)])
    return [("dv", dv_i)]


def emit_ln_apply(p, cst, pp, lnp, x_tiles, out_tiles, t0, t1, wait_rows,
                  gamma_col=None, beta_col=None, sq_view=None, out_full=True):
    w = t1 - t0
    irow, mrow = lnp["irow"], lnp["srow"]
    sq = lnp["sq"] if sq_view is None else sq_view
    dv_prev = p.c("dv")

    def _(t, t0=t0, t1=t1, w=w):
        t.matmul(pp["bc0"][:, 0:w], cst["ones_row_f"][:, :], irow[0:1, t0:t1],
                 start=True, stop=True)
        return t.matmul(pp["bc1"][:, 0:w], cst["ones_row_f"][:, :], mrow[0:1, t0:t1],
                        start=True, stop=True)
    pe_bc = p.emit("tensor", _, waits=[("dv", dv_prev)] + list(wait_rows), inc="pe")

    for dc in range(DC):
        if gamma_col is None:
            def _(v, dc=dc, t0=t0, t1=t1, w=w):
                v.tensor_mul(sq[:, 0:w], x_tiles[dc][:, t0:t1], pp["bc0"][:, 0:w])
                return v.tensor_sub(out_tiles[dc][:, 0:t1 - t0] if out_tiles[dc].shape[-1] == t1 - t0
                                    else out_tiles[dc][:, t0:t1], sq[:, 0:w], pp["bc1"][:, 0:w])
        else:
            def _(v, dc=dc, t0=t0, t1=t1, w=w):
                v.tensor_mul(sq[:, 0:w], x_tiles[dc][:, t0:t1], pp["bc0"][:, 0:w])
                v.tensor_sub(sq[:, 0:w], sq[:, 0:w], pp["bc1"][:, 0:w])
                od = out_tiles[dc][:, 0:t1 - t0] if out_tiles[dc].shape[-1] == t1 - t0 \
                    else out_tiles[dc][:, t0:t1]
                return v.tensor_scalar(od, sq[:, 0:w], gamma_col[dc], beta_col[dc],
                                       ALU.mult, ALU.add)
        p.emit("vector", _, waits=[("pe", pe_bc)])
    return [("dv", p.c("dv"))]


def emit_l1attn(p, cst, pb, A, dT, pfx="m"):
    DI = 2048
    d_x2 = dT["x2"]; d_ipw = dT["ipw"]; d_opw = dT["opw"]; d_xpw = dT["xpw"]
    d_dtw = dT["dtw"]; d_cw = dT["cw"]; d_cb = dT["cb"]; d_dtb = dT["dtb"]
    d_alog = dT["alog"]; d_skd = dT["skd"]; d_nw = dT["nw"]; d_nb = dT["nb"]
    d_fnw = dT["fnw"]; d_fnb = dT["fnb"]; d_x3 = dT["x3"]
    if True:
        pp = pp_roles(pb)
        px1 = pb[6][0:80, 0:TC]
        px2 = pb[7][0:16, 0:TC]

        xm = [A.take((128, T)) for dc in range(DC)]
        USZ = 27648
        u, _uo = A.take_cols(USZ)
        lnp = {"sq": u[:, 0:512], "srow": u[0:1, 512:2560],
               "qrow": u[0:1, 2560:4608], "irow": u[0:1, 4608:6656]}
        CB = 6656
        o_us = CB
        u_s = [u[:, o_us + ec * TC:o_us + (ec + 1) * TC] for ec in range(EC)]
        o_usb = CB + 4096
        usb = [u[:, o_usb + ec * 128:o_usb + (ec + 1) * 128].bitcast(BF16)
               for ec in range(EC)]
        o_zs = CB + 6144
        zs = [u[:, o_zs + ec * 128:o_zs + (ec + 1) * 128].bitcast(BF16)
              for ec in range(EC)]
        o_bbc = CB + 8192
        Bbc = u[:, o_bbc:o_bbc + 2048].bitcast(BF16)
        o_cbc = CB + 10240
        Cbc = u[:, o_cbc:o_cbc + 2048].bitcast(BF16)
        o_ybf = CB + 12288
        y_bf = [u[:, o_ybf + ec * 128:o_ybf + (ec + 1) * 128].bitcast(BF16)
                for ec in range(EC)]
        o_da = CB + 14336
        dA4 = u[:, o_da:o_da + 4 * TC]
        o_dbu = CB + 15360
        dBu4 = u[:, o_dbu:o_dbu + 4 * TC]
        o_h4 = CB + 16384
        h4 = u[:, o_h4:o_h4 + 4 * TC]
        o_dt = CB + 17408
        dt_ec = u[:, o_dt:o_dt + TC]
        o_dtu = CB + 17664
        dtu_ec = u[:, o_dtu:o_dtu + TC]
        o_acc = CB + 17920
        acc_c = u[:, o_acc:o_acc + TC]
        o_ext = CB + 18176
        ext_c = u[:, o_ext:o_ext + TC + 4]
        o_ye = CB + 18436
        y_ec = u[:, o_ye:o_ye + TC]
        o_yt = CB + 18692
        ytmp = u[:, o_yt:o_yt + TC]
        o_sp0 = CB + 18948
        sp0 = u[:, o_sp0:o_sp0 + TC]
        o_sp1 = CB + 19204
        sp1 = u[:, o_sp1:o_sp1 + TC]
        o_dtr = CB + 19460
        dtraw = u[:, o_dtr:o_dtr + 128].bitcast(BF16)[0:64, 0:TC]
        o_bsb = CB + 19588
        B_sb = u[0:16, o_bsb:o_bsb + 128].bitcast(BF16)[0:16, 0:TC]
        o_csb = CB + 19716
        C_sb = u[0:16, o_csb:o_csb + 128].bitcast(BF16)[0:16, 0:TC]
        o_xnc = CB + 19844
        xn_c = [u[:, o_xnc + dc * 128:o_xnc + (dc + 1) * 128].bitcast(BF16)
                for dc in range(DC)]
        chbuf = [u[:, o_us + dc * TB:o_us + (dc + 1) * TB] for dc in range(DC)]
        xchunk = [u[:, o_us + b * TB:o_us + (b + 1) * TB] for b in range(2)]
        obuf = [u[:, o_us + (2 + b) * TB:o_us + (3 + b) * TB] for b in range(2)]

        ipwb = [A.take((128, 1024), BF16) for b in range(2)]
        opwb = [A.take((128, 2048), BF16) for b in range(2)]
        xpwb = A.take((128, EC * 96), BF16)
        dtwb = A.take((64, DI), BF16)
        cwt = A.take((128, EC * 4))
        cbt = A.take((128, EC))
        dtbt = A.take((128, EC))
        skdt = A.take((128, EC))
        nwt = A.take((128, NBLK * DC))
        nbt = A.take((128, NBLK * DC))
        fnwt = A.take((128, DC))
        fnbt = A.take((128, DC))
        negA = A.take((128, EC * NST))
        selT = A.take((16, DI), BF16)
        st_all = A.take((128, NBLK * EC * NST))
        tails = A.take((128, NBLK * EC * 4))

        def _(g):
            g.memset(selT[:, :], 1.0)
            g.affine_select(selT[:, :], selT[:, :], [[1, DI]], ALU.is_ge, 0.0,
                            base=0, channel_multiplier=-128)
            return g.affine_select(selT[:, :], selT[:, :], [[-1, DI]], ALU.is_ge, 0.0,
                                   base=127, channel_multiplier=128)
        p.emit("gpsimd", _)

        gsm = dma_batch(p, "gpsimd",
                        [(nwt[:, b * DC + dc:b * DC + dc + 1], d_nw[b, dc:dc + 1, :]
                          .rearrange("o i -> i o")) for b in range(NBLK) for dc in range(DC)]
                        + [(nbt[:, b * DC + dc:b * DC + dc + 1], d_nb[b, dc:dc + 1, :]
                            .rearrange("o i -> i o")) for b in range(NBLK) for dc in range(DC)]
                        + [(fnwt[:, dc:dc + 1], d_fnw[dc:dc + 1, :].rearrange("o i -> i o"))
                           for dc in range(DC)]
                        + [(fnbt[:, dc:dc + 1], d_fnb[dc:dc + 1, :].rearrange("o i -> i o"))
                           for dc in range(DC)],
                        pfx + "gsm")

        for s in range(NSEQ):
            def _(v):
                v.memset(st_all[:, :], 0.0)
                return v.memset(tails[:, :], 0.0)
            p.emit("vector", _, waits=[("pe", p.c("pe"))])

            for tb in range(NTB):
                t0, t1 = tb * TB, (tb + 1) * TB
                lv = dma_batch(p, "sync",
                               [(chbuf[dc][:, :], d_x2[s, dc * 128:(dc + 1) * 128, t0:t1])
                                for dc in range(DC)],
                               pfx + "ld", waits=[("pe", p.c("pe")), ("dv", p.c("dv")),
                                            ("ac", p.c("ac")), (pfx + "st", p.c(pfx + "st"))])

                def _(t, tb=tb):
                    last = None
                    for dc in range(DC):
                        last = t.matmul(pp["rowA"][0:1, :], cst["ones_col_f"][:, :],
                                        chbuf[dc][:, :], start=(dc == 0), stop=(dc == DC - 1))
                    return last
                pe_v = p.emit("tensor", _, waits=[(pfx + "ld", lv), ("dv", p.c("dv"))]
                              + cst["ready"], inc="pe")

                def _(v, t0=t0, t1=t1):
                    return v.tensor_copy(lnp["srow"][0:1, t0:t1], pp["rowA"][0:1, :])
                p.emit("vector", _, waits=[("pe", pe_v)])
                for dc in range(DC):
                    def _(s_, dc=dc):
                        return s_.activation(lnp["sq"][:, :], chbuf[dc][:, :], AF.Square)
                    ac_v = p.emit("scalar", _, waits=[("pe", p.c("pe"))])

                    def _(t, dc=dc):
                        return t.matmul(pp["rowB"][0:1, :], cst["ones_col_f"][:, :],
                                        lnp["sq"][:, :], start=(dc == 0), stop=(dc == DC - 1))
                    p.emit("tensor", _, waits=[("ac", ac_v), ("dv", p.c("dv"))], inc="pe")

                def _(v, t0=t0, t1=t1):
                    return v.tensor_copy(lnp["qrow"][0:1, t0:t1], pp["rowB"][0:1, :])
                p.emit("vector", _, waits=[("pe", p.c("pe"))])

            def _(v):
                v.tensor_scalar_mul(lnp["srow"][:, :], lnp["srow"][:, :], 1.0 / D)
                v.tensor_scalar_mul(lnp["qrow"][:, :], lnp["qrow"][:, :], 1.0 / D)
                v.tensor_mul(lnp["irow"][:, :], lnp["srow"][:, :], lnp["srow"][:, :])
                return v.tensor_sub(lnp["qrow"][:, :], lnp["qrow"][:, :], lnp["irow"][:, :])
            dv_var = p.emit("vector", _, waits=[("dv", p.c("dv"))])

            def _(s_):
                return s_.activation(lnp["qrow"][:, :], lnp["qrow"][:, :], AF.Sqrt,
                                     bias=cst["eps_row"][:, :])
            ac_std = p.emit("scalar", _, waits=[("dv", dv_var)])

            def _(v):
                v.reciprocal(lnp["irow"][:, :], lnp["qrow"][:, :])
                return v.tensor_mul(lnp["srow"][:, :], lnp["srow"][:, :], lnp["irow"][:, :])
            dv_rows0 = p.emit("vector", _, waits=[("ac", ac_std)])

            for tb in range(NTB):
                t0, t1 = tb * TB, (tb + 1) * TB
                lv = dma_batch(p, "sync",
                               [(chbuf[dc][:, :], d_x2[s, dc * 128:(dc + 1) * 128, t0:t1])
                                for dc in range(DC)],
                               pfx + "ld", waits=[("pe", p.c("pe")), ("dv", p.c("dv")),
                                            ("ac", p.c("ac")), (pfx + "st", p.c(pfx + "st"))])

                def _(t, t0=t0, t1=t1):
                    t.matmul(pp["bc0"][:, :], cst["ones_row_f"][:, :], lnp["irow"][0:1, t0:t1],
                             start=True, stop=True)
                    return t.matmul(pp["bc1"][:, :], cst["ones_row_f"][:, :],
                                    lnp["srow"][0:1, t0:t1], start=True, stop=True)
                pe_bc = p.emit("tensor", _, waits=[(pfx + "ld", lv), ("dv", p.c("dv")),
                                                   ("dv", dv_rows0)], inc="pe")
                for dc in range(DC):
                    def _(v, dc=dc, t0=t0, t1=t1):
                        v.tensor_mul(chbuf[dc][:, :], chbuf[dc][:, :], pp["bc0"][:, :])
                        return v.tensor_sub(xm[dc][:, t0:t1], chbuf[dc][:, :], pp["bc1"][:, :])
                    p.emit("vector", _, waits=[("pe", pe_bc)])

            for b in range(NBLK):
                gb = dma_batch(p, "gpsimd",
                               [(xpwb[:, :], d_xpw[b].rearrange("(eh p) j -> p eh j", p=128)),
                                (dtwb[:, :], d_dtw[b]),
                                (cwt[:, :], d_cw[b].rearrange("(eh p) k -> p eh k", p=128)),
                                (negA[:, :], d_alog[b].rearrange("(eh p) n -> p eh n", p=128))]
                               + [(cbt[:, ec:ec + 1], d_cb[b, ec:ec + 1, :]
                                   .rearrange("o i -> i o")) for ec in range(EC)]
                               + [(dtbt[:, ec:ec + 1], d_dtb[b, ec:ec + 1, :]
                                   .rearrange("o i -> i o")) for ec in range(EC)]
                               + [(skdt[:, ec:ec + 1], d_skd[b, ec:ec + 1, :]
                                   .rearrange("o i -> i o")) for ec in range(EC)],
                               pfx + "gcb", waits=[("ac", p.c("ac")), ("dv", p.c("dv")),
                                             ("pe", p.c("pe"))])

                def _(s_):
                    return s_.activation(negA[:, :], negA[:, :], AF.Exp)
                ac_a = p.emit("scalar", _, waits=[(pfx + "gcb", gb)])

                def _(v):
                    return v.tensor_scalar_mul(negA[:, :], negA[:, :], -1.0)
                dv_a = p.emit("vector", _, waits=[("ac", ac_a)])

                gcol = [nwt[:, b * DC + dc:b * DC + dc + 1] for dc in range(DC)]
                bcol = [nbt[:, b * DC + dc:b * DC + dc + 1] for dc in range(DC)]
                wait_rows = emit_ln_stats(p, cst, pp, lnp, xm,
                                          [(pfx + "gsm", gsm), ("dv", p.c("dv"))])

                for c in range(NCH):
                    c0, c1 = c * TC, (c + 1) * TC
                    wait_xn = emit_ln_apply(p, cst, pp, lnp, xm, xn_c, c0, c1,
                                            wait_rows, gamma_col=gcol, beta_col=bcol,
                                            sq_view=acc_c, out_full=False)

                    for ec in range(EC):
                        par = ec % 2
                        gi = dma_batch(p, "gpsimd",
                                       [(ipwb[par][:, :],
                                         d_ipw[b, ec].rearrange("(dh p) c -> p dh c", p=128))],
                                       pfx + f"gi{par}", waits=[("pe", p.c("pe"))])
                        ps = pp["mm0"] if par == 0 else pp["mm1"]

                        def _(t, ec=ec, par=par, ps=ps):
                            last = None
                            for dc in range(DC):
                                last = t.matmul(ps[:, 0:TC], ipwb[par][:, dc * 128:(dc + 1) * 128],
                                                xn_c[dc][:, :], start=(dc == 0),
                                                stop=(dc == DC - 1))
                            return last
                        pe_u = p.emit("tensor", _,
                                      waits=[(pfx + f"gi{par}", gi), ("dv", p.c("dv")),
                                             ("ac", p.c("ac"))] + list(wait_xn), inc="pe")

                        def _(v, b=b, ec=ec, ps=ps):
                            v.tensor_copy(ext_c[:, 0:4][:, 0:3],
                                          tails[:, (b * EC + ec) * 4:(b * EC + ec) * 4 + 3])
                            v.tensor_copy(ext_c[:, 3:3 + TC], ps[:, 0:TC])
                            v.tensor_copy(tails[:, (b * EC + ec) * 4:(b * EC + ec) * 4 + 3],
                                          ext_c[:, TC:TC + 3])
                            v.tensor_scalar(acc_c[:, :], ext_c[:, 3:3 + TC],
                                            cwt[:, ec * 4 + 3:ec * 4 + 4],
                                            cbt[:, ec:ec + 1], ALU.mult, ALU.add)
                            v.scalar_tensor_tensor(acc_c[:, :], ext_c[:, 0:TC],
                                                   cwt[:, ec * 4:ec * 4 + 1], acc_c[:, :],
                                                   ALU.mult, ALU.add)
                            v.scalar_tensor_tensor(acc_c[:, :], ext_c[:, 1:1 + TC],
                                                   cwt[:, ec * 4 + 1:ec * 4 + 2], acc_c[:, :],
                                                   ALU.mult, ALU.add)
                            return v.scalar_tensor_tensor(acc_c[:, :], ext_c[:, 2:2 + TC],
                                                          cwt[:, ec * 4 + 2:ec * 4 + 3],
                                                          acc_c[:, :], ALU.mult, ALU.add)
                        dv_cv = p.emit("vector", _, waits=[("pe", pe_u), (pfx + "gcb", gb),
                                                           ("ac", p.c("ac"))])

                        def _(s_, ec=ec):
                            return s_.activation(u_s[ec][:, :], acc_c[:, :], AF.Silu)
                        ac_si = p.emit("scalar", _, waits=[("dv", dv_cv)])

                        def _(v, ec=ec):
                            return v.tensor_copy(usb[ec][:, :], u_s[ec][:, :])
                        p.emit("vector", _, waits=[("ac", ac_si)])

                    for ec in range(EC):
                        par = ec % 2
                        gi = dma_batch(p, "gpsimd",
                                       [(ipwb[par][:, :],
                                         d_ipw[b, 16 + ec].rearrange("(dh p) c -> p dh c",
                                                                     p=128))],
                                       pfx + f"gi{par}", waits=[("pe", p.c("pe"))])
                        ps = pp["mm0"] if par == 0 else pp["mm1"]

                        def _(t, ec=ec, par=par, ps=ps):
                            last = None
                            for dc in range(DC):
                                last = t.matmul(ps[:, 0:TC], ipwb[par][:, dc * 128:(dc + 1) * 128],
                                                xn_c[dc][:, :], start=(dc == 0),
                                                stop=(dc == DC - 1))
                            return last
                        pe_z = p.emit("tensor", _,
                                      waits=[(pfx + f"gi{par}", gi), ("dv", p.c("dv")),
                                             ("ac", p.c("ac"))] + list(wait_xn), inc="pe")

                        def _(s_, ec=ec, ps=ps):
                            return s_.activation(zs[ec][:, :], ps[:, 0:TC], AF.Silu)
                        p.emit("scalar", _, waits=[("pe", pe_z)])

                    def _(t):
                        last = None
                        for ec in range(EC):
                            last = t.matmul(px1[:, :], xpwb[:, ec * 96:ec * 96 + 80],
                                            usb[ec][:, :], start=(ec == 0), stop=(ec == EC - 1))
                        for ec in range(EC):
                            last = t.matmul(px2[:, :], xpwb[:, ec * 96 + 80:(ec + 1) * 96],
                                            usb[ec][:, :], start=(ec == 0), stop=(ec == EC - 1))
                        return last
                    pe_xp = p.emit("tensor", _, waits=[("dv", p.c("dv")), (pfx + "gcb", gb),
                                                       ("ac", p.c("ac"))], inc="pe")

                    def _(v):
                        v.tensor_copy(dtraw[:, :], px1[0:64, :])
                        v.tensor_copy(B_sb[:, :], px1[64:80, :])
                        return v.tensor_copy(C_sb[:, :], px2[:, :])
                    dv_xp = p.emit("vector", _, waits=[("pe", pe_xp)])

                    for n in range(NST):
                        psn = pp["bc0"] if n % 2 == 0 else pp["bc1"]

                        def _(t, n=n, psn=psn):
                            t.matmul(psn[:, 0:TC], selT[:, n * 128:(n + 1) * 128],
                                     B_sb[:, :], start=True, stop=True)
                            return t.matmul(psn[:, TC:2 * TC], selT[:, n * 128:(n + 1) * 128],
                                            C_sb[:, :], start=True, stop=True)
                        pe_b = p.emit("tensor", _, waits=[("dv", p.c("dv")), ("dv", dv_xp)],
                                      inc="pe")

                        def _(v, n=n, psn=psn):
                            v.tensor_copy(Bbc[:, n * TC:(n + 1) * TC], psn[:, 0:TC])
                            return v.tensor_copy(Cbc[:, n * TC:(n + 1) * TC], psn[:, TC:2 * TC])
                        p.emit("vector", _, waits=[("pe", pe_b)])

                    for ec in range(EC):
                        def _(t, ec=ec):
                            return t.matmul(pp["mm0"][:, 0:TC],
                                            dtwb[:, ec * 128:(ec + 1) * 128],
                                            dtraw[:, :], start=True, stop=True)
                        pe_dt = p.emit("tensor", _, waits=[("dv", p.c("dv")),
                                                           ("ac", p.c("ac"))], inc="pe")

                        def _(v, ec=ec):
                            return v.tensor_scalar(sp0[:, :], pp["mm0"][:, 0:TC],
                                                   dtbt[:, ec:ec + 1], None, ALU.add)
                        dv_s0 = p.emit("vector", _, waits=[("pe", pe_dt)])

                        def _(s_):
                            return s_.activation(sp1[:, :], sp0[:, :], AF.Abs)
                        ac_s1 = p.emit("scalar", _, waits=[("dv", dv_s0)])

                        def _(s_):
                            return s_.activation(sp1[:, :], sp1[:, :], AF.Exp, scale=-1.0)
                        p.emit("scalar", _)

                        def _(s_):
                            return s_.activation(sp1[:, :], sp1[:, :], AF.Ln,
                                                 bias=cst["ones_col_f"][:, :])
                        ac_s3 = p.emit("scalar", _)

                        def _(v):
                            v.tensor_relu(dt_ec[:, :], sp0[:, :])
                            return v.tensor_add(dt_ec[:, :], dt_ec[:, :], sp1[:, :])
                        dv_dt = p.emit("vector", _, waits=[("ac", ac_s3)])

                        def _(v, ec=ec):
                            return v.tensor_mul(dtu_ec[:, :], dt_ec[:, :], u_s[ec][:, :])
                        dv_dtu = p.emit("vector", _)

                        for g in range(4):
                            def _(s_, b=b, ec=ec, g=g):
                                last = None
                                for j in range(4):
                                    n = 4 * g + j
                                    last = s_.activation(dA4[:, j * TC:(j + 1) * TC],
                                                         dt_ec[:, :], AF.Exp,
                                                         scale=negA[:, ec * NST + n:
                                                                    ec * NST + n + 1])
                                return last
                            ac_da = p.emit("scalar", _, waits=[("dv", dv_dt),
                                                               ("dv", p.c("dv"))])

                            def _(v, g=g):
                                rep = bass.AP(A.t, _uo + o_dtu, [[A.cols, 128], [0, 4], [1, TC]])
                                return v.tensor_tensor(dBu4[:, :], rep,
                                                       Bbc[:, 4 * g * TC:4 * (g + 1) * TC],
                                                       ALU.mult)
                            dv_db = p.emit("vector", _, waits=[("dv", dv_dtu)])

                            def _(v, b=b, ec=ec, g=g):
                                last = None
                                for j in range(4):
                                    n = 4 * g + j
                                    stc = b * EC * NST + ec * NST + n
                                    last = v.tensor_tensor_scan(
                                        h4[:, j * TC:(j + 1) * TC],
                                        dA4[:, j * TC:(j + 1) * TC],
                                        dBu4[:, j * TC:(j + 1) * TC],
                                        st_all[:, stc:stc + 1], ALU.mult, ALU.add)
                                return last
                            p.emit("vector", _, waits=[("ac", ac_da), ("dv", dv_db)])

                            def _(v, b=b, ec=ec, g=g):
                                stc = b * EC * NST + ec * NST + 4 * g
                                hcols = bass.AP(A.t, _uo + o_h4 + TC - 1, [[A.cols, 128], [TC, 4]])
                                return v.tensor_copy(st_all[:, stc:stc + 4], hcols)
                            p.emit("vector", _)

                            def _(v, g=g):
                                return v.tensor_mul(dBu4[:, :], h4[:, :],
                                                    Cbc[:, 4 * g * TC:4 * (g + 1) * TC])
                            p.emit("vector", _)

                            def _(v, g=g):
                                p3 = bass.AP(A.t, _uo + o_dbu, [[A.cols, 128], [1, TC], [TC, 4]])
                                return v.tensor_reduce(ytmp[:, :] if g > 0 else y_ec[:, :],
                                                       p3, AX.X, ALU.add)
                            p.emit("vector", _)
                            if g > 0:
                                def _(v):
                                    return v.tensor_add(y_ec[:, :], y_ec[:, :], ytmp[:, :])
                                p.emit("vector", _)

                        def _(v, ec=ec):
                            v.scalar_tensor_tensor(y_ec[:, :], u_s[ec][:, :],
                                                   skdt[:, ec:ec + 1], y_ec[:, :],
                                                   ALU.mult, ALU.add)
                            return v.tensor_mul(y_bf[ec][:, :], y_ec[:, :], zs[ec][:, :])
                        p.emit("vector", _, waits=[("ac", p.c("ac"))])
                    dv_y = p.c("dv")

                    for dcO in range(DC):
                        par = dcO % 2
                        go = dma_batch(p, "gpsimd",
                                       [(opwb[par][:, :],
                                         d_opw[b, dcO].rearrange("(eh p) c -> p eh c",
                                                                 p=128))],
                                       pfx + f"go{par}", waits=[("pe", p.c("pe"))])
                        ps = pp["mm0"] if par == 0 else pp["mm1"]

                        def _(t, dcO=dcO, par=par, ps=ps):
                            last = None
                            for ec in range(EC):
                                last = t.matmul(ps[:, 0:TC], opwb[par][:, ec * 128:(ec + 1) * 128],
                                                y_bf[ec][:, :], start=(ec == 0),
                                                stop=(ec == EC - 1))
                            return last
                        pe_o = p.emit("tensor", _, waits=[(pfx + f"go{par}", go), ("dv", dv_y),
                                                          ("dv", p.c("dv")),
                                                          ("ac", p.c("ac"))], inc="pe")

                        def _(v, dcO=dcO, c0=c0, c1=c1, ps=ps):
                            return v.tensor_add(xm[dcO][:, c0:c1], xm[dcO][:, c0:c1],
                                                ps[:, 0:TC])
                        p.emit("vector", _, waits=[("pe", pe_o)])

            fg = [fnwt[:, dc:dc + 1] for dc in range(DC)]
            fb = [fnbt[:, dc:dc + 1] for dc in range(DC)]
            wait_rows = emit_ln_stats(p, cst, pp, lnp, xm,
                                      [(pfx + "gsm", gsm), ("dv", p.c("dv"))])
            for tb in range(NTB):
                emit_ln_apply(p, cst, pp, lnp, xm, xm, tb * TB, (tb + 1) * TB,
                              wait_rows, gamma_col=fg, beta_col=fb,
                              sq_view=None, out_full=True)

            for dc in range(DC):
                for tb in range(NTB):
                    t0, t1 = tb * TB, (tb + 1) * TB
                    par = (dc * NTB + tb) % 2
                    lx = dma_batch(p, "sync",
                                   [(xchunk[par][:, :], d_x2[s, dc * 128:(dc + 1) * 128,
                                                             t0:t1])],
                                   pfx + "lxc", waits=[("dv", p.c("dv")), (pfx + "st", p.c(pfx + "st"))])

                    def _(v, dc=dc, t0=t0, t1=t1, par=par):
                        return v.tensor_add(obuf[par][:, :], xchunk[par][:, :],
                                            xm[dc][:, t0:t1])
                    p.emit("vector", _, waits=[(pfx + "lxc", lx), (pfx + "st", p.c(pfx + "st"))])

                    dma_batch(p, "sync",
                              [(d_x3[s, dc * 128:(dc + 1) * 128, t0:t1], obuf[par][:, :])],
                              pfx + "st", waits=[("dv", p.c("dv"))])

    return pfx + "st"


def build_l1attn_launch():
    nc = bass.Bass("TRN2", target_bir_lowering=False)
    DI = 2048
    dT = {
        "x2": nc.dram_tensor("x2", [NSEQ, D, T], F32, kind="ExternalInput"),
        "ipw": nc.dram_tensor("ipw", [NBLK, 32, D, 128], F32, kind="ExternalInput"),
        "opw": nc.dram_tensor("opw", [NBLK, 8, DI, 128], F32, kind="ExternalInput"),
        "xpw": nc.dram_tensor("xpw", [NBLK, DI, 96], F32, kind="ExternalInput"),
        "dtw": nc.dram_tensor("dtw", [NBLK, 64, DI], F32, kind="ExternalInput"),
        "cw": nc.dram_tensor("cw", [NBLK, DI, 4], F32, kind="ExternalInput"),
        "cb": nc.dram_tensor("cb", [NBLK, EC, 128], F32, kind="ExternalInput"),
        "dtb": nc.dram_tensor("dtb", [NBLK, EC, 128], F32, kind="ExternalInput"),
        "alog": nc.dram_tensor("alog", [NBLK, DI, NST], F32, kind="ExternalInput"),
        "skd": nc.dram_tensor("skd", [NBLK, EC, 128], F32, kind="ExternalInput"),
        "nw": nc.dram_tensor("nw", [NBLK, DC, 128], F32, kind="ExternalInput"),
        "nb": nc.dram_tensor("nb", [NBLK, DC, 128], F32, kind="ExternalInput"),
        "fnw": nc.dram_tensor("fnw", [DC, 128], F32, kind="ExternalInput"),
        "fnb": nc.dram_tensor("fnb", [DC, 128], F32, kind="ExternalInput"),
        "x3": nc.dram_tensor("x3", [NSEQ, D, T], F32, kind="ExternalOutput"),
    }
    p = Prog(nc)
    with p.stack:
        cst = common_consts(p)
        pb = make_psum_pool(p)
        A = Arena(p, "arena", 52000)
        st = emit_l1attn(p, cst, pb, A, dT)
        p.emit("sync", lambda y: None, waits=[(st, p.c(st))])
        p.build()
    return nc


def _shards(a, n=8):
    a = np.ascontiguousarray(a)
    return [a[NSEQ * i:NSEQ * (i + 1)] for i in range(n)]


_NC_CACHE = {}


def _get_nc(name, builder):
    if name not in _NC_CACHE:
        _NC_CACHE[name] = builder()
    return _NC_CACHE[name]


def kernel_multi(**inputs):
    f32c = lambda a: np.ascontiguousarray(np.asarray(a, dtype=np.float32))
    x = f32c(inputs["L0_attn_x"])
    wqkv = f32c(inputs["L0_attn_w_qkv"])
    bq = f32c(inputs["L0_attn_b_qkv"]).reshape(1, -1)
    wo = f32c(inputs["L0_attn_w_out"])
    bo = f32c(inputs["L0_attn_b_out"]).reshape(1, -1)
    gwT = np.ascontiguousarray(f32c(inputs["L0_ffn_gate_w"]).T)
    w1 = f32c(inputs["L0_ffn_w1"]); w2 = f32c(inputs["L0_ffn_w2"])
    w3 = f32c(inputs["L0_ffn_w3"])
    ipw = np.ascontiguousarray(f32c(inputs["L1_attn_in_proj_w"])
                               .reshape(4, 1024, 32, 128).transpose(0, 2, 1, 3))
    opw = np.ascontiguousarray(f32c(inputs["L1_attn_out_proj_w"])
                               .reshape(4, 2048, 8, 128).transpose(0, 2, 1, 3))
    xpw = f32c(inputs["L1_attn_xproj_w"])
    dtw = f32c(inputs["L1_attn_dtproj_w"])
    cw = f32c(inputs["L1_attn_conv_w"])
    cb = np.ascontiguousarray(f32c(inputs["L1_attn_conv_b"]).reshape(4, 16, 128))
    dtb = np.ascontiguousarray(f32c(inputs["L1_attn_dtproj_b"]).reshape(4, 16, 128))
    alog = f32c(inputs["L1_attn_A_log"])
    skd = np.ascontiguousarray(f32c(inputs["L1_attn_skip_D"]).reshape(4, 16, 128))
    nw = np.ascontiguousarray(f32c(inputs["L1_attn_norm_w"]).reshape(4, 8, 128))
    nb = np.ascontiguousarray(f32c(inputs["L1_attn_norm_b"]).reshape(4, 8, 128))
    fnw = np.ascontiguousarray(f32c(inputs["L1_attn_fn_w"]).reshape(8, 128))
    fnb = np.ascontiguousarray(f32c(inputs["L1_attn_fn_b"]).reshape(8, 128))
    WeT = np.ascontiguousarray(f32c(inputs["L1_ffn_W_experts"]).transpose(0, 2, 1))
    be = f32c(inputs["L1_ffn_b_experts"])
    WrT = np.ascontiguousarray(f32c(inputs["L1_ffn_W_router"]).T)
    br = f32c(inputs["L1_ffn_b_router"]).reshape(1, 8)

    cores = list(range(8))
    xs = _shards(x)

    nc1 = _get_nc("l0attn", build_l0attn_launch)
    res = run_bass_kernel_spmd(nc1, [
        {"x": xs[i], "wqkv": wqkv, "bq": bq, "wo": wo, "bo": bo} for i in range(8)
    ], core_ids=cores)
    x1s = [np.asarray(res.results[i]["x1"]) for i in range(8)]

    nc2 = _get_nc("l0ffn", build_l0ffn_launch)
    res = run_bass_kernel_spmd(nc2, [
        {"x1": x1s[i], "gwT": gwT, "w1": w1, "w2": w2, "w3": w3} for i in range(8)
    ], core_ids=cores)
    x2s = [np.asarray(res.results[i]["x2"]) for i in range(8)]

    nc3 = _get_nc("l1attn", build_l1attn_launch)
    res = run_bass_kernel_spmd(nc3, [
        {"x2": x2s[i], "ipw": ipw, "opw": opw, "xpw": xpw, "dtw": dtw, "cw": cw,
         "cb": cb, "dtb": dtb, "alog": alog, "skd": skd, "nw": nw, "nb": nb,
         "fnw": fnw, "fnb": fnb} for i in range(8)
    ], core_ids=cores)
    x3s = [np.asarray(res.results[i]["x3"]) for i in range(8)]

    nc4 = _get_nc("l1ffn", build_l1ffn_launch)
    res = run_bass_kernel_spmd(nc4, [
        {"x3": x3s[i], "WeT": WeT, "be": be, "WrT": WrT, "br": br} for i in range(8)
    ], core_ids=cores)
    x4s = [np.asarray(res.results[i]["x4"]) for i in range(8)]

    return np.concatenate(x4s, axis=0)


def build_fused_launch(strict=True):
    nc = bass.Bass("TRN2", target_bir_lowering=False)
    NE = 8
    DI = 2048
    din = lambda n, s: nc.dram_tensor(n, s, F32, kind="ExternalInput")
    dint = lambda n, s: nc.dram_tensor(n, s, F32)
    d = {}
    d["x"] = din("x", [NSEQ, T, D])
    d["wqkv"] = din("wqkv", [D, 3 * D]); d["bq"] = din("bq", [1, 3 * D])
    d["wo"] = din("wo", [D, D]); d["bo"] = din("bo", [1, D])
    d["gwT"] = din("gwT", [D, NE])
    d["w1"] = din("w1", [NE, D, HID]); d["w2"] = din("w2", [NE, HID, D])
    d["w3"] = din("w3", [NE, D, HID])
    d["ipw"] = din("ipw", [NBLK, 32, D, 128]); d["opw"] = din("opw", [NBLK, 8, DI, 128])
    d["xpw"] = din("xpw", [NBLK, DI, 96]); d["dtw"] = din("dtw", [NBLK, 64, DI])
    d["cw"] = din("cw", [NBLK, DI, 4]); d["cb"] = din("cb", [NBLK, EC, 128])
    d["dtb"] = din("dtb", [NBLK, EC, 128]); d["alog"] = din("alog", [NBLK, DI, NST])
    d["skd"] = din("skd", [NBLK, EC, 128])
    d["nw"] = din("nw", [NBLK, DC, 128]); d["nb"] = din("nb", [NBLK, DC, 128])
    d["fnw"] = din("fnw", [DC, 128]); d["fnb"] = din("fnb", [DC, 128])
    d["WeT"] = din("WeT", [NE, D, D]); d["be"] = din("be", [NE, D])
    d["WrT"] = din("WrT", [D, NE]); d["br"] = din("br", [1, NE])
    d["x4"] = nc.dram_tensor("x4", [NSEQ, T, D], F32, kind="ExternalOutput")
    d["xfm_scr"] = dint("xfm_scr", [NSEQ, D, T])
    d["x1i"] = dint("x1i", [NSEQ, D, T])
    d["x2i"] = dint("x2i", [NSEQ, D, T])
    d["x3i"] = dint("x3i", [NSEQ, D, T])

    p = Prog(nc, strict_chains=strict)
    with p.stack:
        cst = common_consts(p)
        pb = make_psum_pool(p)
        A = Arena(p, "arena", 52000)

        st_a = emit_l0attn(p, cst, pb, A,
                           {"x": d["x"], "wqkv": d["wqkv"], "bq": d["bq"],
                            "wo": d["wo"], "bo": d["bo"], "xfm_scr": d["xfm_scr"],
                            "x1": d["x1i"]}, pfx="a")
        emit_fence(p, dma_sems=[st_a])
        A.reset()
        st_f0 = emit_l0ffn(p, cst, pb, A,
                           {"x1": d["x1i"], "gwT": d["gwT"], "w1": d["w1"],
                            "w2": d["w2"], "w3": d["w3"], "x2": d["x2i"]}, pfx="f0")
        emit_fence(p, dma_sems=[st_f0])
        A.reset()
        st_m = emit_l1attn(p, cst, pb, A,
                           {"x2": d["x2i"], "ipw": d["ipw"], "opw": d["opw"],
                            "xpw": d["xpw"], "dtw": d["dtw"], "cw": d["cw"],
                            "cb": d["cb"], "dtb": d["dtb"], "alog": d["alog"],
                            "skd": d["skd"], "nw": d["nw"], "nb": d["nb"],
                            "fnw": d["fnw"], "fnb": d["fnb"], "x3": d["x3i"]}, pfx="m")
        emit_fence(p, dma_sems=[st_m])
        A.reset()
        st_f1 = emit_l1ffn(p, cst, pb, A,
                           {"x3": d["x3i"], "WeT": d["WeT"], "be": d["be"],
                            "WrT": d["WrT"], "br": d["br"], "x4": d["x4"]}, pfx="f1")
        p.emit("sync", lambda y: None, waits=[(st_f1, p.c(st_f1))])
        p.build()
    return nc


def kernel_fused_inmap(inputs):
    f32c = lambda a: np.ascontiguousarray(np.asarray(a, dtype=np.float32))
    m = {}
    m["wqkv"] = f32c(inputs["L0_attn_w_qkv"])
    m["bq"] = f32c(inputs["L0_attn_b_qkv"]).reshape(1, -1)
    m["wo"] = f32c(inputs["L0_attn_w_out"])
    m["bo"] = f32c(inputs["L0_attn_b_out"]).reshape(1, -1)
    m["gwT"] = np.ascontiguousarray(f32c(inputs["L0_ffn_gate_w"]).T)
    m["w1"] = f32c(inputs["L0_ffn_w1"]); m["w2"] = f32c(inputs["L0_ffn_w2"])
    m["w3"] = f32c(inputs["L0_ffn_w3"])
    m["ipw"] = np.ascontiguousarray(f32c(inputs["L1_attn_in_proj_w"])
                                    .reshape(4, 1024, 32, 128).transpose(0, 2, 1, 3))
    m["opw"] = np.ascontiguousarray(f32c(inputs["L1_attn_out_proj_w"])
                                    .reshape(4, 2048, 8, 128).transpose(0, 2, 1, 3))
    m["xpw"] = f32c(inputs["L1_attn_xproj_w"])
    m["dtw"] = f32c(inputs["L1_attn_dtproj_w"])
    m["cw"] = f32c(inputs["L1_attn_conv_w"])
    m["cb"] = np.ascontiguousarray(f32c(inputs["L1_attn_conv_b"]).reshape(4, 16, 128))
    m["dtb"] = np.ascontiguousarray(f32c(inputs["L1_attn_dtproj_b"]).reshape(4, 16, 128))
    m["alog"] = f32c(inputs["L1_attn_A_log"])
    m["skd"] = np.ascontiguousarray(f32c(inputs["L1_attn_skip_D"]).reshape(4, 16, 128))
    m["nw"] = np.ascontiguousarray(f32c(inputs["L1_attn_norm_w"]).reshape(4, 8, 128))
    m["nb"] = np.ascontiguousarray(f32c(inputs["L1_attn_norm_b"]).reshape(4, 8, 128))
    m["fnw"] = np.ascontiguousarray(f32c(inputs["L1_attn_fn_w"]).reshape(8, 128))
    m["fnb"] = np.ascontiguousarray(f32c(inputs["L1_attn_fn_b"]).reshape(8, 128))
    m["WeT"] = np.ascontiguousarray(f32c(inputs["L1_ffn_W_experts"]).transpose(0, 2, 1))
    m["be"] = f32c(inputs["L1_ffn_b_experts"])
    m["WrT"] = np.ascontiguousarray(f32c(inputs["L1_ffn_W_router"]).T)
    m["br"] = f32c(inputs["L1_ffn_b_router"]).reshape(1, 8)
    return m


def kernel_fused(**inputs):
    x = np.ascontiguousarray(np.asarray(inputs["L0_attn_x"], dtype=np.float32))
    wmap = kernel_fused_inmap(inputs)
    xs = _shards(x)
    nc = _get_nc("fused", lambda: build_fused_launch(strict=True))
    res = run_bass_kernel_spmd(nc, [dict(wmap, x=xs[i]) for i in range(8)],
                               core_ids=list(range(8)))
    return np.concatenate([np.asarray(res.results[i]["x4"]) for i in range(8)], axis=0)


def kernel(**inputs):
    return kernel_fused(**inputs)
